# version 01
# baseline (speedup 1.0000x reference)
import sys

import numpy as np
import ml_dtypes

if "/opt/trn_rl_repo" not in sys.path:
    sys.path.insert(0, "/opt/trn_rl_repo")

import concourse.bass as bass
import concourse.mybir as mybir
from concourse.bass_utils import run_bass_kernel_spmd

F32 = mybir.dt.float32
BF16 = mybir.dt.bfloat16
AF = mybir.ActivationFunctionType
ALU = mybir.AluOpType
AX = mybir.AxisListType
BFNP = ml_dtypes.bfloat16

NCORE = 8
D = 2048
NTOK = 8192
TPC = NTOK // NCORE
S = 4096
HD = 128
NHC = 2
DT = D // 128
F = 1408
FT = F // 128
BLK = 512
NB = NTOK // BLK
QBN = S // BLK
SCALE = float(1.0 / np.sqrt(128.0))
KB = 1024

SEM_NAMES = ["sP", "sV", "sA",
             "dPre", "dX", "dAgx", "dAgt", "dMask",
             "dXn0", "dXn1", "dOtW0", "dOtW1", "dStg0", "dStg1",
             "dCl0", "dCl1", "dCo0", "dCo1", "dDbg",
             "dQkv", "dGate", "dUp", "dDown", "dWo",
             "cAgx", "cAgt", "cRs"]


def build_program(debug=False, upto=None):
    nc = bass.Bass("TRN2", target_bir_lowering=False, debug=False)

    def din(name, shape, dt=F32):
        return nc.dram_tensor(name, shape, dt, kind="ExternalInput").ap()

    xT = din("xT", [D, TPC])
    wqT = din("wqT", [D, NHC * HD])
    wkT = din("wkT", [D, NHC * HD])
    wvT = din("wvT", [D, NHC * HD])
    woT = din("woT", [NHC * HD, D])
    bqh = din("bqh", [HD, NHC])
    bkh = din("bkh", [HD, NHC])
    bvr = din("bvr", [1, NHC * HD])
    boh = din("boh", [128, DT])
    gwT = din("gwT", [D, 8])
    gTd = din("gTd", [D, F])
    uTd = din("uTd", [D, F])
    dTd = din("dTd", [F, D])
    selb = din("selb", [8, 128])
    masks = din("masks", [4, 128, BLK], BF16)
    idf32 = din("idf32", [128, 128])

    outT = nc.dram_tensor("outT", [D, TPC], F32, kind="ExternalOutput").ap()

    agin_x = nc.dram_tensor("agin_x", [D, TPC], BF16)
    agout_x = nc.dram_tensor("agout_x", [NCORE * D, TPC], BF16)
    agin_t = nc.dram_tensor("agin_t", [8, TPC], F32)
    agout_t = nc.dram_tensor("agout_t", [NCORE * 8, TPC], F32)
    ot_d = nc.dram_tensor("ot_d", [NHC * HD, NTOK], BF16)
    part = nc.dram_tensor("part", [NCORE * D, TPC], F32)
    rsout = nc.dram_tensor("rsout", [D, TPC], F32)

    dbg = {}
    if debug:
        for nm, t in [("d_agx", agout_x), ("d_agt", agout_t), ("d_ot", ot_d),
                      ("d_part", part), ("d_rs", rsout)]:
            dbg[nm] = (nc.dram_tensor(nm, list(t.ap().shape), t.ap().dtype,
                                      kind="ExternalOutput").ap(), t)

    ARENA = 205 * KB
    arena = nc.alloc_sbuf_tensor("arena", [128, ARENA], mybir.dt.uint8)

    def cv(off, nelem, dt, parts=128):
        nb = {F32: 4, BF16: 2}[dt]
        assert off % 64 == 0 and off + nelem * nb <= ARENA, f"bad carve {off}"
        return arena.ap()[0:parts, off:off + nelem * nb].bitcast(dt)

    o = 0
    onesm = cv(o, 128, F32); o += 512
    ones_col_bf = cv(o, 1, BF16); o += 64
    ones_row = cv(o, 128, F32, parts=1); o += 512
    neg8 = cv(o, 1, F32); o += 64
    eps128 = cv(o, 1, F32); o += 64
    bq_sb = cv(o, 2, F32); o += 64
    bk_sb = cv(o, 2, F32); o += 64
    bo_sb = cv(o, DT, F32); o += 64
    bv_bc = cv(o, NHC * HD, BF16); o += 512
    bvr_sb = cv(o, NHC * HD, F32, parts=1); o += 1024
    idf32_sb = cv(o, 128, F32); o += 512
    gwT_sb = cv(o, DT * 8, F32); o += 512
    selb_sb = cv(o, 128, F32, parts=8); o += 512
    assert o <= 6 * KB

    xnb = [cv(6 * KB + i * 16 * KB, DT * BLK, BF16) for i in range(2)]
    pt_t = [cv(6 * KB + i * KB, BLK, BF16) for i in range(4)]
    mask_sb = [cv(10 * KB + i * KB, BLK, BF16) for i in range(4)]
    otv = [cv(14 * KB + i * KB, BLK, BF16) for i in range(2)]
    rec_bc = cv(16 * KB, BLK, BF16)
    rrow = [cv(17 * KB + 2048 * i, BLK, F32, parts=1) for i in range(2)]

    QT_sb = cv(38 * KB, NHC * NTOK, BF16)
    KT_sb = cv(70 * KB, NHC * NTOK, BF16)
    V_sb = cv(102 * KB, 64 * NHC * HD, BF16)
    xblk = cv(38 * KB, DT * BLK, F32)
    xn32 = cv(70 * KB, DT * BLK, F32)
    sqt = [cv(102 * KB + i * 2 * KB, BLK, F32) for i in range(2)]
    rstd_t = cv(106 * KB, BLK, F32)
    twt = [cv(108 * KB + i * 2 * KB, BLK, F32, parts=8) for i in range(2)]
    uT_sb = cv(38 * KB, DT * F, BF16)
    dT_sb = cv(82 * KB, FT * D, BF16)
    wo_sb = cv(126 * KB, NHC * D, BF16)
    c_rs = [cv(38 * KB + i * 2 * KB, BLK, F32) for i in range(2)]
    c_x = [cv(42 * KB + i * 2 * KB, BLK, F32) for i in range(2)]
    c_o = [cv(46 * KB + i * 2 * KB, BLK, F32) for i in range(2)]

    gT_sb = cv(134 * KB, DT * F, BF16)

    wq_sb = cv(178 * KB, DT * NHC * HD, BF16)
    wk_sb = cv(186 * KB, DT * NHC * HD, BF16)
    wv_sb = cv(194 * KB, DT * NHC * HD, BF16)
    hp_t = [cv(178 * KB + i * KB, BLK, BF16) for i in range(FT)]
    otin = [cv(189 * KB + i * KB, BLK, BF16) for i in range(4)]
    stg = [cv(193 * KB + i * 2 * KB, BLK, F32) for i in range(2)]
    sg_t = [cv(197 * KB + i * KB, BLK, BF16) for i in range(2)]
    twbc = [cv(199 * KB + i * KB, BLK, BF16) for i in range(2)]
    twtile = [cv(201 * KB + i * 2 * KB, BLK, F32, parts=8) for i in range(2)]

    pb = [nc.alloc_psum_tensor(f"pb{i}", [128, BLK], F32) for i in range(8)]

    C = {k: 0 for k in SEM_NAMES}
    sem = {k: nc.semaphore(k).__enter__() for k in SEM_NAMES}

    def inc(name, ins, n=None):
        if n is None:
            n = 1 if name in ("sP", "sV", "sA") or name.startswith("c") else 16
        ins.then_inc(sem[name], n)
        C[name] += n
        return C[name]

    M = {}
    block = nc.Block().__enter__()

    wqv = wq_sb.rearrange("p (t h) -> p t h", t=DT)
    wkv = wk_sb.rearrange("p (t h) -> p t h", t=DT)
    wvv = wv_sb.rearrange("p (t h) -> p t h", t=DT)
    gwv = gwT_sb.rearrange("p (t e) -> p t e", t=DT)
    gv = gT_sb.rearrange("p (t f) -> p t f", t=DT)
    uv = uT_sb.rearrange("p (t f) -> p t f", t=DT)
    dvw = dT_sb.rearrange("p (t d) -> p t d", t=FT)
    wov = wo_sb.rearrange("p (h d) -> p h d", h=NHC)
    xnv = [bb.rearrange("p (t q) -> p t q", t=DT) for bb in xnb]

    def _(q):
        inc("dPre", q.dma_start(out=idf32_sb, in_=idf32))
        inc("dPre", q.dma_start(out=bq_sb, in_=bqh))
        inc("dPre", q.dma_start(out=bk_sb, in_=bkh))
        inc("dPre", q.dma_start(out=bo_sb, in_=boh))
        inc("dPre", q.dma_start(out=bvr_sb, in_=bvr))
        inc("dPre", q.dma_start(out=gwv, in_=gwT.rearrange("(t p) e -> p t e", p=128)))
        inc("dPre", q.dma_start(out=selb_sb, in_=selb))
    block.sync(_)
    PRE_ALL = C["dPre"]

    def _(q):
        inc("dQkv", q.dma_start(out=wqv, in_=wqT.rearrange("(t p) h -> p t h", p=128)))
        inc("dQkv", q.dma_start(out=wkv, in_=wkT.rearrange("(t p) h -> p t h", p=128)))
        inc("dQkv", q.dma_start(out=wvv, in_=wvT.rearrange("(t p) h -> p t h", p=128)))
        inc("dGate", q.dma_start(out=gv, in_=gTd.rearrange("(t p) f -> p t f", p=128)))
    block.gpsimd(_)

    def _(v):
        v.memset(onesm, 1.0)
        v.memset(ones_col_bf, 1.0)
        v.memset(ones_row, 1.0)
        v.memset(neg8, -8.0)
        M["consts"] = inc("sV", v.memset(eps128, 1e-5))
    block.vector(_)

    def _(t):
        t.wait_ge(sem["dPre"], PRE_ALL)
        t.wait_ge(sem["sV"], M["consts"])
        M["bv_pe"] = inc("sP", t.matmul(pb[7][:, 0:NHC * HD], ones_row, bvr_sb,
                                        start=True, stop=True))
    block.tensor(_)

    def _(v):
        v.wait_ge(sem["sP"], M["bv_pe"])
        M["bv"] = inc("sV", v.tensor_copy(bv_bc, pb[7][:, 0:NHC * HD]))
    block.vector(_)

    for j in range(2):
        def _(q, j=j):
            if j > 0:
                q.wait_ge(sem["sP"], M[("Arpe", j - 1, 3)])
            inc("dX", q.dma_start(
                out=xblk.rearrange("p (t q) -> p t q", t=DT),
                in_=xT.rearrange("(t p) q -> p t q", p=128)[:, :, j * BLK:(j + 1) * BLK]))
        block.sync(_)

        def _(t, j=j):
            t.wait_ge(sem["dX"], (j + 1) * 16)
            t.wait_ge(sem["sV"], M["consts"])
            if j > 0:
                t.wait_ge(sem["sV"], M[("Axn1", j - 1)])
            for i in range(DT):
                ins = t.matmul(pb[0][:, :], onesm,
                               xblk[:, i * BLK:(i + 1) * BLK],
                               start=(i == 0), stop=(i == DT - 1))
            M[("Asum", j)] = inc("sP", ins)
        block.tensor(_)

        for i in range(DT):
            def _(sc, j=j, i=i):
                if i == 0:
                    sc.wait_ge(sem["dX"], (j + 1) * 16)
                    if j > 0:
                        sc.wait_ge(sem["sV"], M[("Atw", j - 1, 3)])
                        sc.wait_ge(sem["sP"], M[("AtwT", j - 1, 3)])
                if i >= 2:
                    sc.wait_ge(sem["sP"], M[("Asqpe", j, i - 2)])
                M[("Asq", j, i)] = inc("sA", sc.activation(
                    sqt[i % 2], xblk[:, i * BLK:(i + 1) * BLK], AF.Square))
            block.scalar(_)

            def _(t, j=j, i=i):
                t.wait_ge(sem["sA"], M[("Asq", j, i)])
                M[("Asqpe", j, i)] = inc("sP", t.matmul(
                    pb[1][:, :], onesm, sqt[i % 2],
                    start=(i == 0), stop=(i == DT - 1)))
            block.tensor(_)

        def _(v, j=j):
            v.wait_ge(sem["sP"], M[("Asum", j)])
            v.wait_ge(sem["sP"], M[("Asqpe", j, DT - 1)])
            w = inc("sV", v.tensor_scalar_mul(sqt[0], pb[0][:, :], 1.0 / D))
            w = inc("sV", v.tensor_scalar_mul(sqt[1], pb[1][:, :], 1.0 / D))
            v.wait_ge(sem["sV"], w)
            w = inc("sV", v.tensor_mul(sqt[0], sqt[0], sqt[0]))
            v.wait_ge(sem["sV"], w)
            M[("Avar", j)] = inc("sV", v.tensor_sub(sqt[1], sqt[1], sqt[0]))
        block.vector(_)

        def _(sc, j=j):
            sc.wait_ge(sem["sV"], M[("Avar", j)])
            M[("Astd", j)] = inc("sA", sc.activation(rstd_t, sqt[1], AF.Sqrt,
                                                     bias=eps128, scale=1.0))
        block.scalar(_)

        def _(v, j=j):
            v.wait_ge(sem["sA"], M[("Astd", j)])
            M[("Arstd", j)] = inc("sV", v.reciprocal(rstd_t, rstd_t))
            v.wait_ge(sem["sV"], M[("Arstd", j)])
            for i in range(DT):
                ins = v.scalar_tensor_tensor(xn32[:, i * BLK:(i + 1) * BLK],
                                             pb[0][:, :], -1.0 / D,
                                             xblk[:, i * BLK:(i + 1) * BLK],
                                             ALU.mult, ALU.add)
            w = inc("sV", ins)
            v.wait_ge(sem["sV"], w)
            for i in range(DT):
                ins = v.tensor_mul(xn32[:, i * BLK:(i + 1) * BLK],
                                   xn32[:, i * BLK:(i + 1) * BLK], rstd_t)
            M[("Axn1", j)] = inc("sV", ins)
            v.wait_ge(sem["sV"], M[("Axn1", j)])
            for i in range(DT):
                ins = v.tensor_copy(xnb[j][:, i * BLK:(i + 1) * BLK],
                                    xn32[:, i * BLK:(i + 1) * BLK])
            M[("Axncp", j)] = inc("sV", ins)
        block.vector(_)

        def _(q, j=j):
            q.wait_ge(sem["sV"], M[("Axncp", j)])
            inc("dAgx", q.dma_start(
                out=agin_x.ap().rearrange("(t p) q -> p t q", p=128)[:, :, j * BLK:(j + 1) * BLK],
                in_=xnv[j]))
        block.sync(_)

        for tt in range(4):
            def _(t, j=j, tt=tt):
                if tt == 0:
                    t.wait_ge(sem["dPre"], PRE_ALL)
                    t.wait_ge(sem["sV"], M[("Axn1", j)])
                prev = (j, tt - 1) if tt > 0 else ((j - 1, 3) if j > 0 else None)
                if prev is not None:
                    t.wait_ge(sem["sV"], M[("ArL", prev[0], prev[1])])
                for i in range(DT):
                    ins = t.matmul(pb[4][0:128, 0:8],
                                   xn32[:, i * BLK + tt * 128:i * BLK + tt * 128 + 128],
                                   gwv[:, i, :], start=(i == 0), stop=(i == DT - 1))
                M[("Arpe", j, tt)] = inc("sP", ins)
            block.tensor(_)

            def _(v, j=j, tt=tt):
                L = sqt[0][:, 0:8]; m1 = sqt[0][:, 8:9]; k1 = sqt[0][:, 16:24]
                L2 = sqt[0][:, 24:32]; m2 = sqt[0][:, 32:33]; k2 = sqt[0][:, 40:48]
                dd = sqt[0][:, 48:49]
                v.wait_ge(sem["sP"], M[("Arpe", j, tt)])
                M[("ArL", j, tt)] = w = inc("sV", v.tensor_copy(L, pb[4][0:128, 0:8]))
                v.wait_ge(sem["sV"], w)
                w = inc("sV", v.reduce_max(m1, L, axis=AX.X))
                v.wait_ge(sem["sV"], w)
                w = inc("sV", v.tensor_scalar(k1, L, m1, None, ALU.is_equal))
                v.wait_ge(sem["sV"], w)
                w = inc("sV", v.scalar_tensor_tensor(L2, k1, -1e30, L, ALU.mult, ALU.add))
                v.wait_ge(sem["sV"], w)
                w = inc("sV", v.reduce_max(m2, L2, axis=AX.X))
                v.wait_ge(sem["sV"], w)
                v.tensor_scalar(k2, L2, m2, None, ALU.is_equal)
                M[("Ar1", j, tt)] = inc("sV", v.tensor_sub(dd, m2, m1))
            block.vector(_)

            def _(sc, j=j, tt=tt):
                sc.wait_ge(sem["sV"], M[("Ar1", j, tt)])
                M[("Ar2", j, tt)] = inc("sA", sc.activation(
                    sqt[0][:, 49:50], sqt[0][:, 48:49], AF.Exp))
            block.scalar(_)

            def _(v, j=j, tt=tt):
                k1 = sqt[0][:, 16:24]; k2 = sqt[0][:, 40:48]
                ee = sqt[0][:, 49:50]; rr = sqt[0][:, 50:51]; w2 = sqt[0][:, 51:52]
                tw = sqt[1][:, 0:8]
                v.wait_ge(sem["sA"], M[("Ar2", j, tt)])
                w = inc("sV", v.tensor_scalar_add(rr, ee, 1.0))
                v.wait_ge(sem["sV"], w)
                w = inc("sV", v.reciprocal(rr, rr))
                v.wait_ge(sem["sV"], w)
                w = inc("sV", v.tensor_mul(w2, ee, rr))
                v.wait_ge(sem["sV"], w)
                prev = (j, tt - 1) if tt > 0 else ((j - 1, 3) if j > 0 else None)
                if prev is not None:
                    v.wait_ge(sem["sP"], M[("AtwT", prev[0], prev[1])])
                w = inc("sV", v.tensor_scalar(tw, k1, rr, None, ALU.mult))
                v.wait_ge(sem["sV"], w)
                M[("Atw", j, tt)] = inc("sV", v.scalar_tensor_tensor(
                    tw, k2, w2, tw, ALU.mult, ALU.add))
            block.vector(_)

            def _(t, j=j, tt=tt):
                t.wait_ge(sem["sV"], M[("Atw", j, tt)])
                M[("AtwT", j, tt)] = inc("sP", t.transpose(
                    pb[5][0:8, 0:128], sqt[1][:, 0:8], idf32_sb))
            block.tensor(_)

            def _(v, j=j, tt=tt):
                v.wait_ge(sem["sP"], M[("AtwT", j, tt)])
                M[("Atwc", j, tt)] = inc("sV", v.tensor_copy(
                    twt[j][0:8, tt * 128:(tt + 1) * 128], pb[5][0:8, 0:128]))
            block.vector(_)

        def _(q, j=j):
            q.wait_ge(sem["sV"], M[("Atwc", j, 3)])
            inc("dAgt", q.dma_start(out=agin_t.ap()[0:8, j * BLK:(j + 1) * BLK],
                                    in_=twt[j][0:8, :]))
        block.sync(_)

    if upto == "A":
        dbgA = {}
        for nm, shape, dt in [("t_xblk", [128, DT * BLK], F32),
                              ("t_xn32", [128, DT * BLK], F32),
                              ("t_srow", [128, BLK], F32),
                              ("t_agx", [D, TPC], BF16),
                              ("t_agt", [8, TPC], F32)]:
            dbgA[nm] = nc.dram_tensor(nm, shape, dt, kind="ExternalOutput").ap()

        def _(q):
            q.wait_ge(sem["sV"], M[("Axncp", 1)])
            q.wait_ge(sem["sV"], M[("Atwc", 1, 3)])
            q.wait_ge(sem["dAgx"], C["dAgx"])
            q.wait_ge(sem["dAgt"], C["dAgt"])
            inc("dDbg", q.dma_start(out=dbgA["t_xblk"], in_=xblk))
            inc("dDbg", q.dma_start(out=dbgA["t_xn32"], in_=xn32))
            inc("dDbg", q.dma_start(out=dbgA["t_srow"], in_=rstd_t))
            inc("dDbg", q.dma_start(out=dbgA["t_agx"], in_=agin_x.ap()))
            inc("dDbg", q.dma_start(out=dbgA["t_agt"], in_=agin_t.ap()))
            q.wait_ge(sem["dDbg"], C["dDbg"])
            q.wait_ge(sem["dQkv"], C["dQkv"])
            q.wait_ge(sem["dGate"], C["dGate"])
        block.sync(_)
        block.__exit__(None, None, None)
        return nc

    def _(q):
        q.wait_ge(sem["dAgx"], 32)
        q.wait_ge(sem["dAgt"], 32)
        inc("cAgx", q.collective_compute(
            "AllGather", ALU.bypass, replica_groups=[list(range(NCORE))],
            ins=[agin_x.ap().opt()], outs=[agout_x.ap().opt()]), 1)
        inc("cAgt", q.collective_compute(
            "AllGather", ALU.bypass, replica_groups=[list(range(NCORE))],
            ins=[agin_t.ap().opt()], outs=[agout_t.ap().opt()]), 1)
    block.gpsimd(_)

    def xn_load(q, tb):
        p = tb % 2
        r, half = tb // 2, tb % 2
        src = agout_x.ap()[r * D:(r + 1) * D, half * BLK:(half + 1) * BLK]
        inc(f"dXn{p}", q.dma_start(out=xnv[p], in_=src.rearrange("(t p) q -> p t q", p=128)))

    for tb in range(NB):
        p = tb % 2

        def _(q, tb=tb, p=p):
            if tb == 0:
                q.wait_ge(sem["cAgx"], 1)
            if tb >= 2:
                q.wait_ge(sem["sP"], M[("B1pe", tb - 2)])
            xn_load(q, tb)
            M[("B1x", tb)] = C[f"dXn{p}"]
        block.sync(_)

        def _(t, tb=tb, p=p):
            t.wait_ge(sem[f"dXn{p}"], M[("B1x", tb)])
            if tb == 0:
                t.wait_ge(sem["dQkv"], 48)
            if tb >= 1:
                t.wait_ge(sem["sA"], M[("B1evA", tb - 1)])
                t.wait_ge(sem["sV"], M[("B1evV", tb - 1)])
            xv = xnv[p]
            for h in range(NHC):
                for i in range(DT):
                    t.matmul(pb[0 + h][:, :], wqv[:, i, h * HD:(h + 1) * HD],
                             xv[:, i, :], start=(i == 0), stop=(i == DT - 1))
                for i in range(DT):
                    t.matmul(pb[2 + h][:, :], wkv[:, i, h * HD:(h + 1) * HD],
                             xv[:, i, :], start=(i == 0), stop=(i == DT - 1))
            for s4 in range(4):
                bank, half = 4 + s4 // 2, s4 % 2
                for i in range(DT):
                    ins = t.matmul(pb[bank][:, half * 256:(half + 1) * 256],
                                   xv[:, i, s4 * 128:(s4 + 1) * 128],
                                   wvv[:, i, :], start=(i == 0), stop=(i == DT - 1))
            M[("B1pe", tb)] = inc("sP", ins)
        block.tensor(_)

        def _(sc, tb=tb):
            sc.wait_ge(sem["sP"], M[("B1pe", tb)])
            if tb == 0:
                sc.wait_ge(sem["dPre"], PRE_ALL)
            for h in range(NHC):
                sc.activation(QT_sb[:, h * NTOK + tb * BLK:h * NTOK + (tb + 1) * BLK],
                              pb[0 + h][:, :], AF.Identity, bias=bq_sb[:, h:h + 1],
                              scale=1.0)
                ins = sc.activation(KT_sb[:, h * NTOK + tb * BLK:h * NTOK + (tb + 1) * BLK],
                                    pb[2 + h][:, :], AF.Identity, bias=bk_sb[:, h:h + 1],
                                    scale=1.0)
            M[("B1evA", tb)] = inc("sA", ins)
        block.scalar(_)

        def _(v, tb=tb):
            v.wait_ge(sem["sP"], M[("B1pe", tb)])
            if tb == 0:
                v.wait_ge(sem["sV"], M["bv"])
            for s4 in range(4):
                bank, half = 4 + s4 // 2, s4 % 2
                g = tb * 4 + s4
                ins = v.tensor_add(V_sb[:, g * 256:(g + 1) * 256],
                                   pb[bank][:, half * 256:(half + 1) * 256], bv_bc)
            M[("B1evV", tb)] = inc("sV", ins)
        block.vector(_)

    def _(q):
        q.wait_ge(sem["sP"], M[("B1pe", 14)])
        for m in range(4):
            inc("dMask", q.dma_start(out=mask_sb[m], in_=masks[m]))
    block.sync(_)

    baseP, baseA, baseV = C["sP"], C["sA"], C["sV"]
    sM, pvM, bcM, expM, maskM = {}, {}, {}, {}, {}
    rrowM, recM, recbcM, otvM = {}, {}, {}, {}
    otdM = {}
    p_, a_, v_ = baseP, baseA, baseV
    ot_use = {0: C["dOtW0"], 1: C["dOtW1"]}
    gq = 0
    g = 0
    for u in range(4):
        for qb in range(QBN):
            nk = 4 * qb + 4
            for kt in range(nk):
                sM[g] = p_ + 1
                pvM[g] = p_ + 2
                p_ += 2
                expM[g] = a_ + 1
                a_ += 1
                if kt >= 4 * qb:
                    maskM[g] = v_ + 1
                    v_ += 1
                g += 1
            bcM[gq] = p_ + 1
            p_ += 1
            rrowM[gq] = v_ + 1
            recM[gq] = v_ + 2
            recbcM[gq] = v_ + 3
            otvM[gq] = v_ + 4
            v_ += 4
            ot_use[gq % 2] += 16
            otdM[gq] = ot_use[gq % 2]
            gq += 1

    gq = 0
    g = 0
    for u in range(4):
        b, hh = u // 2, u % 2
        qoff = hh * NTOK + b * S
        for qb in range(QBN):
            nk = 4 * qb + 4
            g0, gq0 = g, gq

            def _(sc, u=u, qb=qb, nk=nk, g0=g0):
                for kt in range(nk):
                    gg = g0 + kt
                    sc.wait_ge(sem["sP"], sM[gg])
                    if gg >= 4:
                        sc.wait_ge(sem["sP"], pvM[gg - 4])
                    ins = sc.activation(pt_t[gg % 4], pb[gg % 2][:, :], AF.Exp,
                                        bias=neg8, scale=SCALE)
                    assert inc("sA", ins) == expM[gg]
            block.scalar(_)

            def _(v, u=u, qb=qb, nk=nk, g0=g0, gq0=gq0):
                if u == 0 and qb == 0:
                    v.wait_ge(sem["dMask"], 64)
                for kt in range(4 * qb, nk):
                    gg = g0 + kt
                    v.wait_ge(sem["sA"], expM[gg])
                    ins = v.tensor_mul(pt_t[gg % 4], pt_t[gg % 4], mask_sb[kt - 4 * qb])
                    assert inc("sV", ins) == maskM[gg]
                rbank = 3 if gq0 % 2 == 0 else 6
                obank = 2 if gq0 % 2 == 0 else 5
                v.wait_ge(sem["sP"], pvM[g0 + nk - 1])
                if gq0 >= 2:
                    v.wait_ge(sem["sP"], bcM[gq0 - 2])
                assert inc("sV", v.tensor_copy(rrow[gq0 % 2], pb[rbank][0:1, :])) == rrowM[gq0]
                v.wait_ge(sem["sV"], rrowM[gq0])
                assert inc("sV", v.reciprocal(rrow[gq0 % 2], rrow[gq0 % 2])) == recM[gq0]
                v.wait_ge(sem["sP"], bcM[gq0])
                assert inc("sV", v.tensor_copy(rec_bc, pb[4][:, :])) == recbcM[gq0]
                v.wait_ge(sem["sV"], recbcM[gq0])
                if gq0 >= 2:
                    v.wait_ge(sem[f"dOtW{gq0 % 2}"], otdM[gq0 - 2])
                ins = v.tensor_mul(otv[gq0 % 2], pb[obank][:, :], rec_bc)
                assert inc("sV", ins) == otvM[gq0]
            block.vector(_)

            def _(t, u=u, qb=qb, nk=nk, g0=g0, gq0=gq0, hh=hh, b=b, qoff=qoff):
                obank = 2 if gq0 % 2 == 0 else 5
                rbank = 3 if gq0 % 2 == 0 else 6
                if u == 0 and qb == 0:
                    t.wait_ge(sem["sA"], M[("B1evA", NB - 1)])
                    t.wait_ge(sem["sV"], M[("B1evV", NB - 1)])
                if gq0 >= 2:
                    t.wait_ge(sem["sV"], otvM[gq0 - 2])
                for kt in range(nk):
                    gg = g0 + kt
                    if gg >= 2:
                        t.wait_ge(sem["sA"], expM[gg - 2])
                    ins = t.matmul(pb[gg % 2][:, :],
                                   KT_sb[:, qoff + kt * HD:qoff + (kt + 1) * HD],
                                   QT_sb[:, qoff + qb * BLK:qoff + (qb + 1) * BLK],
                                   start=True, stop=True)
                    assert inc("sP", ins) == sM[gg]
                    t.wait_ge(sem["sA"], expM[gg])
                    if kt >= 4 * qb:
                        t.wait_ge(sem["sV"], maskM[gg])
                    t.matmul(pb[obank][:, :],
                             V_sb[:, (b * 32 + kt) * 256 + hh * HD:(b * 32 + kt) * 256 + (hh + 1) * HD],
                             pt_t[gg % 4], start=(kt == 0), stop=(kt == nk - 1))
                    ins = t.matmul(pb[rbank][0:1, :], ones_col_bf, pt_t[gg % 4],
                                   start=(kt == 0), stop=(kt == nk - 1))
                    assert inc("sP", ins) == pvM[gg]
                t.wait_ge(sem["sV"], recM[gq0])
                if gq0 >= 1:
                    t.wait_ge(sem["sV"], recbcM[gq0 - 1])
                ins = t.matmul(pb[4][:, :], ones_row, rrow[gq0 % 2], start=True, stop=True)
                assert inc("sP", ins) == bcM[gq0]
            block.tensor(_)

            def _(q, gq0=gq0, hh=hh, b=b, qb=qb):
                q.wait_ge(sem["sV"], otvM[gq0])
                ins = q.dma_start(
                    out=ot_d.ap()[hh * HD:(hh + 1) * HD,
                                  b * S + qb * BLK:b * S + (qb + 1) * BLK],
                    in_=otv[gq0 % 2])
                assert inc(f"dOtW{gq0 % 2}", ins) == otdM[gq0]
            block.sync(_)

            g += nk
            gq += 1

    M["B2peEnd"] = C["sP"]
    M["B2vEnd"] = C["sV"]
    OTW_ALL = (C["dOtW0"], C["dOtW1"])

    def _(q):
        q.wait_ge(sem["sP"], M["B2peEnd"])
        inc("dUp", q.dma_start(out=uv, in_=uTd.rearrange("(t p) f -> p t f", p=128)))
        inc("dDown", q.dma_start(out=dvw, in_=dTd.rearrange("(t p) d -> p t d", p=128)))
        inc("dWo", q.dma_start(out=wov, in_=woT.rearrange("(h p) d -> p h d", p=128)))
    block.gpsimd(_)

    for tb in range(NB):
        p = tb % 2
        r, half = tb // 2, tb % 2

        def _(q, tb=tb, p=p, r=r, half=half):
            if tb == 0:
                q.wait_ge(sem["sP"], M["B2peEnd"])
                q.wait_ge(sem["sV"], M["B2vEnd"])
                q.wait_ge(sem["dOtW0"], OTW_ALL[0])
                q.wait_ge(sem["dOtW1"], OTW_ALL[1])
                q.wait_ge(sem["cAgt"], 1)
            if tb >= 2:
                q.wait_ge(sem["sP"], M[("B3gu", tb - 2, FT - 1)])
                q.wait_ge(sem["sP"], M[("B3dn", tb - 2, DT - 1)])
            xn_load(q, tb)
            for hh in range(NHC):
                inc(f"dXn{p}", q.dma_start(
                    out=otin[2 * p + hh],
                    in_=ot_d.ap()[hh * HD:(hh + 1) * HD, tb * BLK:(tb + 1) * BLK]))
            inc(f"dXn{p}", q.dma_start(
                out=twtile[p],
                in_=agout_t.ap()[r * 8:(r + 1) * 8, half * BLK:(half + 1) * BLK]))
            M[("B3ld", tb)] = C[f"dXn{p}"]
        block.sync(_)

        def _(t, tb=tb, p=p):
            t.wait_ge(sem[f"dXn{p}"], M[("B3ld", tb)])
            if tb == 0:
                t.wait_ge(sem["dGate"], 16)
                t.wait_ge(sem["sV"], M["B2vEnd"])
            if tb >= 1:
                t.wait_ge(sem["sV"], M[("B3twc", tb - 1)])
            M[("B3twpe", tb)] = inc("sP", t.matmul(
                pb[4][:, :], selb_sb, twtile[p], start=True, stop=True))
        block.tensor(_)

        def _(v, tb=tb, p=p):
            v.wait_ge(sem["sP"], M[("B3twpe", tb)])
            M[("B3twc", tb)] = inc("sV", v.tensor_copy(twbc[p], pb[4][:, :]))
        block.vector(_)

        for ft in range(FT):
            def _(t, tb=tb, ft=ft, p=p):
                if tb > 0 or ft > 0:
                    pft = (tb, ft - 1) if ft > 0 else (tb - 1, FT - 1)
                    t.wait_ge(sem["sA"], M[("B3sg", pft[0], pft[1])])
                    t.wait_ge(sem["sV"], M[("B3h", pft[0], pft[1])])
                xv = xnv[p]
                for i in range(DT):
                    t.matmul(pb[0][:, :], gv[:, i, ft * HD:(ft + 1) * HD],
                             xv[:, i, :], start=(i == 0), stop=(i == DT - 1))
                if ft == 0 and tb == 0:
                    t.wait_ge(sem["dUp"], 16)
                for i in range(DT):
                    ins = t.matmul(pb[1][:, :], uv[:, i, ft * HD:(ft + 1) * HD],
                                   xv[:, i, :], start=(i == 0), stop=(i == DT - 1))
                M[("B3gu", tb, ft)] = inc("sP", ins)
            block.tensor(_)

            def _(sc, tb=tb, ft=ft):
                sc.wait_ge(sem["sP"], M[("B3gu", tb, ft)])
                M[("B3sg", tb, ft)] = inc("sA", sc.activation(
                    sg_t[ft % 2], pb[0][:, :], AF.Silu))
            block.scalar(_)

            def _(v, tb=tb, ft=ft, p=p):
                v.wait_ge(sem["sA"], M[("B3sg", tb, ft)])
                if tb >= 1 and ft == 0:
                    v.wait_ge(sem["sP"], M[("B3dn", tb - 1, DT - 1)])
                w = inc("sV", v.tensor_mul(hp_t[ft], pb[1][:, :], sg_t[ft % 2]))
                v.wait_ge(sem["sV"], w)
                M[("B3h", tb, ft)] = inc("sV", v.tensor_mul(
                    hp_t[ft], hp_t[ft], twbc[p]))
            block.vector(_)

        for d in range(DT):
            def _(t, tb=tb, d=d, p=p):
                bank = 2 + d % 2
                if d == 0:
                    t.wait_ge(sem["sV"], M[("B3h", tb, FT - 1)])
                    if tb == 0:
                        t.wait_ge(sem["dDown"], 16)
                        t.wait_ge(sem["dWo"], 16)
                if tb > 0 or d >= 2:
                    pd = (tb, d - 2) if d >= 2 else (tb - 1, DT - 2 + d)
                    t.wait_ge(sem["sV"], M[("B3ev", pd[0], pd[1])])
                for ft in range(FT):
                    t.matmul(pb[bank][:, :], dvw[:, ft, d * HD:(d + 1) * HD],
                             hp_t[ft], start=(ft == 0), stop=False)
                for hh in range(NHC):
                    ins = t.matmul(pb[bank][:, :], wov[:, hh, d * HD:(d + 1) * HD],
                                   otin[2 * p + hh],
                                   start=False, stop=(hh == NHC - 1))
                M[("B3dn", tb, d)] = inc("sP", ins)
            block.tensor(_)

            def _(v, tb=tb, d=d):
                bank = 2 + d % 2
                v.wait_ge(sem["sP"], M[("B3dn", tb, d)])
                if tb > 0 or d >= 2:
                    pd = (tb, d - 2) if d >= 2 else (tb - 1, DT - 2 + d)
                    v.wait_ge(sem[f"dStg{d % 2}"], M[("B3st", pd[0], pd[1])])
                M[("B3ev", tb, d)] = inc("sV", v.tensor_copy(stg[d % 2], pb[bank][:, :]))
            block.vector(_)

            def _(q, tb=tb, d=d, r=r, half=half):
                q.wait_ge(sem["sV"], M[("B3ev", tb, d)])
                ins = q.dma_start(
                    out=part.ap()[r * D + d * HD:r * D + (d + 1) * HD,
                                  half * BLK:(half + 1) * BLK],
                    in_=stg[d % 2])
                M[("B3st", tb, d)] = inc(f"dStg{d % 2}", ins)
            block.sync(_)

    def _(q):
        q.wait_ge(sem["dStg0"], C["dStg0"])
        q.wait_ge(sem["dStg1"], C["dStg1"])
        inc("cRs", q.collective_compute(
            "ReduceScatter", ALU.add, replica_groups=[list(range(NCORE))],
            ins=[part.ap().opt()], outs=[rsout.ap().opt()]), 1)
    block.gpsimd(_)

    it = 0
    for d in range(DT):
        for j2 in range(2):
            p = it % 2

            def _(q, d=d, j2=j2, it=it, p=p):
                if it == 0:
                    q.wait_ge(sem["cRs"], 1)
                if it >= 2:
                    q.wait_ge(sem["sA"], M[("Cb", it - 2)])
                    q.wait_ge(sem[f"dCo{p}"], M[("Co", it - 2)])
                inc(f"dCl{p}", q.dma_start(
                    out=c_rs[p],
                    in_=rsout.ap()[d * HD:(d + 1) * HD, j2 * BLK:(j2 + 1) * BLK]))
                inc(f"dCl{p}", q.dma_start(
                    out=c_x[p],
                    in_=xT[d * HD:(d + 1) * HD, j2 * BLK:(j2 + 1) * BLK]))
                M[("Cl", it)] = C[f"dCl{p}"]
            block.sync(_)

            def _(v, d=d, it=it, p=p):
                v.wait_ge(sem[f"dCl{p}"], M[("Cl", it)])
                M[("Ca", it)] = inc("sV", v.tensor_add(c_o[p], c_rs[p], c_x[p]))
            block.vector(_)

            def _(sc, d=d, it=it, p=p):
                sc.wait_ge(sem["sV"], M[("Ca", it)])
                M[("Cb", it)] = inc("sA", sc.activation(
                    c_o[p], c_o[p], AF.Identity,
                    bias=bo_sb[:, d:d + 1], scale=1.0))
            block.scalar(_)

            def _(q, d=d, j2=j2, it=it, p=p):
                q.wait_ge(sem["sA"], M[("Cb", it)])
                ins = q.dma_start(
                    out=outT[d * HD:(d + 1) * HD, j2 * BLK:(j2 + 1) * BLK],
                    in_=c_o[p])
                M[("Co", it)] = inc(f"dCo{p}", ins)
            block.sync(_)
            it += 1

    if debug:
        def _(q):
            for nm, (dst, src) in dbg.items():
                inc("dDbg", q.dma_start(out=dst, in_=src.ap()))
        block.sync(_)

    def _(q):
        for k in SEM_NAMES:
            if (k.startswith("d") or k.startswith("c")) and C[k]:
                q.wait_ge(sem[k], C[k])
    block.sync(_)

    block.__exit__(None, None, None)
    return nc


def prep_inputs(L0_attn_h, L0_attn_Wq, L0_attn_bq, L0_attn_Wk, L0_attn_bk,
                L0_attn_Wv, L0_attn_bv, L0_attn_Wo, L0_attn_bo,
                L0_ffn_gate_w, L0_ffn_gate_proj, L0_ffn_up_proj, L0_ffn_down_proj):
    x = np.ascontiguousarray(np.asarray(L0_attn_h, dtype=np.float32).reshape(NTOK, D).T)
    wqT = np.ascontiguousarray(np.asarray(L0_attn_Wq).T)
    wkT = np.ascontiguousarray(np.asarray(L0_attn_Wk).T)
    wvT = np.ascontiguousarray(np.asarray(L0_attn_Wv).T)
    woT = np.ascontiguousarray(np.asarray(L0_attn_Wo).T)
    gwT = np.ascontiguousarray(np.asarray(L0_ffn_gate_w).T)
    bq = np.asarray(L0_attn_bq); bk = np.asarray(L0_attn_bk)
    bv = np.asarray(L0_attn_bv); bo = np.asarray(L0_attn_bo)
    gp = np.asarray(L0_ffn_gate_proj); up = np.asarray(L0_ffn_up_proj)
    dp = np.asarray(L0_ffn_down_proj)

    mk = np.zeros((4, 128, BLK), dtype=BFNP)
    for m in range(4):
        i = np.arange(128)[:, None]
        jj = np.arange(BLK)[None, :]
        mk[m] = (128 * m + i <= jj).astype(BFNP)
    idf = np.eye(128, dtype=np.float32)

    in_maps = []
    for c in range(NCORE):
        hsl = slice(2 * c * HD, (2 * c + 2) * HD)
        sel = np.zeros((8, 128), np.float32)
        sel[c, :] = 1.0
        in_maps.append({
            "xT": np.ascontiguousarray(x[:, c * TPC:(c + 1) * TPC]),
            "wqT": np.ascontiguousarray(wqT[:, hsl]),
            "wkT": np.ascontiguousarray(wkT[:, hsl]),
            "wvT": np.ascontiguousarray(wvT[:, hsl]),
            "woT": np.ascontiguousarray(woT[hsl, :]),
            "bqh": np.ascontiguousarray(bq[hsl].reshape(NHC, HD).T),
            "bkh": np.ascontiguousarray(bk[hsl].reshape(NHC, HD).T),
            "bvr": np.ascontiguousarray(bv[hsl].reshape(1, NHC * HD)),
            "boh": np.ascontiguousarray(bo.reshape(DT, 128).T),
            "gwT": gwT,
            "gTd": np.ascontiguousarray(gp[c].T),
            "uTd": np.ascontiguousarray(up[c].T),
            "dTd": np.ascontiguousarray(dp[c].T),
            "selb": sel,
            "masks": mk,
            "idf32": idf,
        })
    return in_maps


_CACHE = {}


def kernel(**inputs):
    if "nc" not in _CACHE:
        _CACHE["nc"] = build_program(debug=False)
    nc = _CACHE["nc"]
    in_maps = prep_inputs(**inputs)
    res = run_bass_kernel_spmd(nc, in_maps, core_ids=list(range(NCORE)))
    shards = [r["outT"] for r in res.results]
    outT = np.concatenate(shards, axis=1)
    return np.ascontiguousarray(outT.T).reshape(2, S, D).astype(np.float32)


# version 02
# speedup vs baseline: 1.0808x; 1.0808x over previous
import sys

import numpy as np
import ml_dtypes

if "/opt/trn_rl_repo" not in sys.path:
    sys.path.insert(0, "/opt/trn_rl_repo")

import concourse.bass as bass
import concourse.mybir as mybir
from concourse.bass_utils import run_bass_kernel_spmd

F32 = mybir.dt.float32
BF16 = mybir.dt.bfloat16
AF = mybir.ActivationFunctionType
ALU = mybir.AluOpType
AX = mybir.AxisListType
BFNP = ml_dtypes.bfloat16

NCORE = 8
D = 2048
NTOK = 8192
TPC = NTOK // NCORE
S = 4096
HD = 128
NHC = 2
DT = D // 128
F = 1408
FT = F // 128
BLK = 512
NB = NTOK // BLK
QBN = S // BLK
SCALE = float(1.0 / np.sqrt(128.0))
KB = 1024

SEM_NAMES = ["sP", "sV", "sA",
             "dPre", "dX", "dAgx", "dAgt", "dMask",
             "dXn0", "dXn1", "dOtW0", "dOtW1", "dStg0", "dStg1",
             "dCl0", "dCl1", "dCl2", "dCl3", "dCo0", "dCo1", "dCo2", "dCo3", "dDbg",
             "dQkv", "dGate", "dUp", "dDown", "dWo",
             "cAgx", "cAgt", "cRsA", "cRsB"]


def build_program(debug=False, upto=None):
    nc = bass.Bass("TRN2", target_bir_lowering=False, debug=False)

    def din(name, shape, dt=F32):
        return nc.dram_tensor(name, shape, dt, kind="ExternalInput").ap()

    xT = din("xT", [D, TPC])
    wqT = din("wqT", [D, NHC * HD])
    wkT = din("wkT", [D, NHC * HD])
    wvT = din("wvT", [D, NHC * HD])
    woT = din("woT", [NHC * HD, D])
    bqh = din("bqh", [HD, NHC])
    bkh = din("bkh", [HD, NHC])
    bvr = din("bvr", [1, NHC * HD])
    boh = din("boh", [128, DT])
    gwT = din("gwT", [D, 8])
    gTd = din("gTd", [D, F])
    uTd = din("uTd", [D, F])
    dTd = din("dTd", [F, D])
    selb = din("selb", [8, 128])
    masks = din("masks", [4, 128, BLK], BF16)
    idf32 = din("idf32", [128, 128])

    outT = nc.dram_tensor("outT", [D, TPC], F32, kind="ExternalOutput").ap()

    agin_x = nc.dram_tensor("agin_x", [D, TPC], BF16)
    agout_x = nc.dram_tensor("agout_x", [NCORE * D, TPC], BF16)
    agin_t = nc.dram_tensor("agin_t", [8, TPC], F32)
    agout_t = nc.dram_tensor("agout_t", [NCORE * 8, TPC], F32)
    ot_d = nc.dram_tensor("ot_d", [NHC * HD, NTOK], BF16)
    part = [nc.dram_tensor(f"part{h}", [NCORE * D, BLK], F32) for h in range(2)]
    rsout = [nc.dram_tensor(f"rsout{h}", [D, BLK], F32) for h in range(2)]

    dbg = {}
    if debug:
        for nm, t in [("d_agx", agout_x), ("d_agt", agout_t), ("d_ot", ot_d),
                      ("d_part0", part[0]), ("d_part1", part[1]),
                      ("d_rs0", rsout[0]), ("d_rs1", rsout[1])]:
            dbg[nm] = (nc.dram_tensor(nm, list(t.ap().shape), t.ap().dtype,
                                      kind="ExternalOutput").ap(), t)

    ARENA = 205 * KB
    arena = nc.alloc_sbuf_tensor("arena", [128, ARENA], mybir.dt.uint8)

    def cv(off, nelem, dt, parts=128):
        nb = {F32: 4, BF16: 2}[dt]
        assert off % 64 == 0 and off + nelem * nb <= ARENA, f"bad carve {off}"
        return arena.ap()[0:parts, off:off + nelem * nb].bitcast(dt)

    o = 0
    onesm = cv(o, 128, F32); o += 512
    ones_col_bf = cv(o, 1, BF16); o += 64
    ones_row = cv(o, 128, F32, parts=1); o += 512
    neg8 = cv(o, 1, F32); o += 64
    eps128 = cv(o, 1, F32); o += 64
    bq_sb = cv(o, 2, F32); o += 64
    bk_sb = cv(o, 2, F32); o += 64
    bo_sb = cv(o, DT, F32); o += 64
    bv_bc = cv(o, NHC * HD, BF16); o += 512
    bvr_sb = cv(o, NHC * HD, F32, parts=1); o += 1024
    idf32_sb = cv(o, 128, F32); o += 512
    gwT_sb = cv(o, DT * 8, F32); o += 512
    selb_sb = cv(o, 128, F32, parts=8); o += 512
    assert o <= 6 * KB

    xnb = [cv(6 * KB + i * 16 * KB, DT * BLK, BF16) for i in range(2)]
    pt_t = [cv(6 * KB + i * KB, BLK, BF16) for i in range(4)]
    mask_sb = [cv(10 * KB + i * KB, BLK, BF16) for i in range(4)]
    otv = [cv(14 * KB + i * KB, BLK, BF16) for i in range(2)]
    rec_bc = cv(16 * KB, BLK, BF16)
    rrow = [cv(17 * KB + 2048 * i, BLK, F32, parts=1) for i in range(2)]

    QT_sb = cv(38 * KB, NHC * NTOK, BF16)
    KT_sb = cv(70 * KB, NHC * NTOK, BF16)
    V_sb = cv(102 * KB, 64 * NHC * HD, BF16)
    xblk = cv(38 * KB, DT * BLK, F32)
    xn32 = cv(70 * KB, DT * BLK, F32)
    sqt = [cv(102 * KB + i * 2 * KB, BLK, F32) for i in range(2)]
    rstd_t = cv(106 * KB, BLK, F32)
    twt = [cv(108 * KB + i * 2 * KB, BLK, F32, parts=8) for i in range(2)]
    uT_sb = cv(38 * KB, DT * F, BF16)
    dT_sb = cv(82 * KB, FT * D, BF16)
    wo_sb = cv(126 * KB, NHC * D, BF16)
    c_rs = [cv(38 * KB + i * 2 * KB, BLK, F32) for i in range(4)]
    c_x = [cv(46 * KB + i * 2 * KB, BLK, F32) for i in range(4)]
    c_o = [cv(54 * KB + i * 2 * KB, BLK, F32) for i in range(4)]

    gT_sb = cv(134 * KB, DT * F, BF16)

    wq_sb = cv(178 * KB, DT * NHC * HD, BF16)
    wk_sb = cv(186 * KB, DT * NHC * HD, BF16)
    wv_sb = cv(194 * KB, DT * NHC * HD, BF16)
    hp_t = [cv(178 * KB + i * KB, BLK, BF16) for i in range(FT)]
    otin = [cv(189 * KB + i * KB, BLK, BF16) for i in range(4)]
    stg = [cv(193 * KB + i * 2 * KB, BLK, F32) for i in range(2)]
    sg_t = [cv(197 * KB + i * KB, BLK, BF16) for i in range(2)]
    twbc = [cv(199 * KB + i * KB, BLK, BF16) for i in range(2)]
    twtile = [cv(201 * KB + i * 2 * KB, BLK, F32, parts=8) for i in range(2)]

    pb = [nc.alloc_psum_tensor(f"pb{i}", [128, BLK], F32) for i in range(8)]

    C = {k: 0 for k in SEM_NAMES}
    sem = {k: nc.semaphore(k).__enter__() for k in SEM_NAMES}

    def inc(name, ins, n=None):
        if n is None:
            n = 1 if name in ("sP", "sV", "sA") or name.startswith("c") else 16
        ins.then_inc(sem[name], n)
        C[name] += n
        return C[name]

    M = {}
    block = nc.Block().__enter__()

    wqv = wq_sb.rearrange("p (t h) -> p t h", t=DT)
    wkv = wk_sb.rearrange("p (t h) -> p t h", t=DT)
    wvv = wv_sb.rearrange("p (t h) -> p t h", t=DT)
    gwv = gwT_sb.rearrange("p (t e) -> p t e", t=DT)
    gv = gT_sb.rearrange("p (t f) -> p t f", t=DT)
    uv = uT_sb.rearrange("p (t f) -> p t f", t=DT)
    dvw = dT_sb.rearrange("p (t d) -> p t d", t=FT)
    wov = wo_sb.rearrange("p (h d) -> p h d", h=NHC)
    xnv = [bb.rearrange("p (t q) -> p t q", t=DT) for bb in xnb]

    def _(q):
        inc("dPre", q.dma_start(out=idf32_sb, in_=idf32))
        inc("dPre", q.dma_start(out=bq_sb, in_=bqh))
        inc("dPre", q.dma_start(out=bk_sb, in_=bkh))
        inc("dPre", q.dma_start(out=bo_sb, in_=boh))
        inc("dPre", q.dma_start(out=bvr_sb, in_=bvr))
        inc("dPre", q.dma_start(out=gwv, in_=gwT.rearrange("(t p) e -> p t e", p=128)))
        inc("dPre", q.dma_start(out=selb_sb, in_=selb))
    block.sync(_)
    PRE_ALL = C["dPre"]

    def _(q):
        inc("dQkv", q.dma_start(out=wqv, in_=wqT.rearrange("(t p) h -> p t h", p=128)))
        inc("dQkv", q.dma_start(out=wkv, in_=wkT.rearrange("(t p) h -> p t h", p=128)))
        inc("dQkv", q.dma_start(out=wvv, in_=wvT.rearrange("(t p) h -> p t h", p=128)))
        inc("dGate", q.dma_start(out=gv, in_=gTd.rearrange("(t p) f -> p t f", p=128)))
    block.gpsimd(_)

    def _(v):
        v.memset(onesm, 1.0)
        v.memset(ones_col_bf, 1.0)
        v.memset(ones_row, 1.0)
        v.memset(neg8, -8.0)
        M["consts"] = inc("sV", v.memset(eps128, 1e-5))
    block.vector(_)

    def _(t):
        t.wait_ge(sem["dPre"], PRE_ALL)
        t.wait_ge(sem["sV"], M["consts"])
        M["bv_pe"] = inc("sP", t.matmul(pb[7][:, 0:NHC * HD], ones_row, bvr_sb,
                                        start=True, stop=True))
    block.tensor(_)

    def _(v):
        v.wait_ge(sem["sP"], M["bv_pe"])
        M["bv"] = inc("sV", v.tensor_copy(bv_bc, pb[7][:, 0:NHC * HD]))
    block.vector(_)

    for j in range(2):
        def _(q, j=j):
            if j > 0:
                q.wait_ge(sem["sP"], M[("Arpe", j - 1, 3)])
            inc("dX", q.dma_start(
                out=xblk.rearrange("p (t q) -> p t q", t=DT),
                in_=xT.rearrange("(t p) q -> p t q", p=128)[:, :, j * BLK:(j + 1) * BLK]))
        block.sync(_)

        def _(t, j=j):
            t.wait_ge(sem["dX"], (j + 1) * 16)
            t.wait_ge(sem["sV"], M["consts"])
            if j > 0:
                t.wait_ge(sem["sV"], M[("Axn1", j - 1)])
            for i in range(DT):
                ins = t.matmul(pb[0][:, :], onesm,
                               xblk[:, i * BLK:(i + 1) * BLK],
                               start=(i == 0), stop=(i == DT - 1))
            M[("Asum", j)] = inc("sP", ins)
        block.tensor(_)

        for i in range(DT):
            def _(sc, j=j, i=i):
                if i == 0:
                    sc.wait_ge(sem["dX"], (j + 1) * 16)
                    if j > 0:
                        sc.wait_ge(sem["sV"], M[("Atw", j - 1, 3)])
                        sc.wait_ge(sem["sP"], M[("AtwT", j - 1, 3)])
                if i >= 2:
                    sc.wait_ge(sem["sP"], M[("Asqpe", j, i - 2)])
                M[("Asq", j, i)] = inc("sA", sc.activation(
                    sqt[i % 2], xblk[:, i * BLK:(i + 1) * BLK], AF.Square))
            block.scalar(_)

            def _(t, j=j, i=i):
                t.wait_ge(sem["sA"], M[("Asq", j, i)])
                M[("Asqpe", j, i)] = inc("sP", t.matmul(
                    pb[1][:, :], onesm, sqt[i % 2],
                    start=(i == 0), stop=(i == DT - 1)))
            block.tensor(_)

        def _(v, j=j):
            v.wait_ge(sem["sP"], M[("Asum", j)])
            v.wait_ge(sem["sP"], M[("Asqpe", j, DT - 1)])
            w = inc("sV", v.tensor_scalar_mul(sqt[0], pb[0][:, :], 1.0 / D))
            w = inc("sV", v.tensor_scalar_mul(sqt[1], pb[1][:, :], 1.0 / D))
            v.wait_ge(sem["sV"], w)
            w = inc("sV", v.tensor_mul(sqt[0], sqt[0], sqt[0]))
            v.wait_ge(sem["sV"], w)
            M[("Avar", j)] = inc("sV", v.tensor_sub(sqt[1], sqt[1], sqt[0]))
        block.vector(_)

        def _(sc, j=j):
            sc.wait_ge(sem["sV"], M[("Avar", j)])
            M[("Astd", j)] = inc("sA", sc.activation(rstd_t, sqt[1], AF.Sqrt,
                                                     bias=eps128, scale=1.0))
        block.scalar(_)

        def _(v, j=j):
            v.wait_ge(sem["sA"], M[("Astd", j)])
            M[("Arstd", j)] = inc("sV", v.reciprocal(rstd_t, rstd_t))
            v.wait_ge(sem["sV"], M[("Arstd", j)])
            for i in range(DT):
                ins = v.scalar_tensor_tensor(xn32[:, i * BLK:(i + 1) * BLK],
                                             pb[0][:, :], -1.0 / D,
                                             xblk[:, i * BLK:(i + 1) * BLK],
                                             ALU.mult, ALU.add)
            w = inc("sV", ins)
            v.wait_ge(sem["sV"], w)
            for i in range(DT):
                ins = v.tensor_mul(xn32[:, i * BLK:(i + 1) * BLK],
                                   xn32[:, i * BLK:(i + 1) * BLK], rstd_t)
            M[("Axn1", j)] = inc("sV", ins)
            v.wait_ge(sem["sV"], M[("Axn1", j)])
            for i in range(DT):
                ins = v.tensor_copy(xnb[j][:, i * BLK:(i + 1) * BLK],
                                    xn32[:, i * BLK:(i + 1) * BLK])
            M[("Axncp", j)] = inc("sV", ins)
        block.vector(_)

        def _(q, j=j):
            q.wait_ge(sem["sV"], M[("Axncp", j)])
            inc("dAgx", q.dma_start(
                out=agin_x.ap().rearrange("(t p) q -> p t q", p=128)[:, :, j * BLK:(j + 1) * BLK],
                in_=xnv[j]))
        block.sync(_)

        for tt in range(4):
            def _(t, j=j, tt=tt):
                if tt == 0:
                    t.wait_ge(sem["dPre"], PRE_ALL)
                    t.wait_ge(sem["sV"], M[("Axn1", j)])
                prev = (j, tt - 1) if tt > 0 else ((j - 1, 3) if j > 0 else None)
                if prev is not None:
                    t.wait_ge(sem["sV"], M[("ArL", prev[0], prev[1])])
                for i in range(DT):
                    ins = t.matmul(pb[4][0:128, 0:8],
                                   xn32[:, i * BLK + tt * 128:i * BLK + tt * 128 + 128],
                                   gwv[:, i, :], start=(i == 0), stop=(i == DT - 1))
                M[("Arpe", j, tt)] = inc("sP", ins)
            block.tensor(_)

            def _(v, j=j, tt=tt):
                L = sqt[0][:, 0:8]; m1 = sqt[0][:, 8:9]; k1 = sqt[0][:, 16:24]
                L2 = sqt[0][:, 24:32]; m2 = sqt[0][:, 32:33]; k2 = sqt[0][:, 40:48]
                dd = sqt[0][:, 48:49]
                v.wait_ge(sem["sP"], M[("Arpe", j, tt)])
                M[("ArL", j, tt)] = w = inc("sV", v.tensor_copy(L, pb[4][0:128, 0:8]))
                v.wait_ge(sem["sV"], w)
                w = inc("sV", v.reduce_max(m1, L, axis=AX.X))
                v.wait_ge(sem["sV"], w)
                w = inc("sV", v.tensor_scalar(k1, L, m1, None, ALU.is_equal))
                v.wait_ge(sem["sV"], w)
                w = inc("sV", v.scalar_tensor_tensor(L2, k1, -1e30, L, ALU.mult, ALU.add))
                v.wait_ge(sem["sV"], w)
                w = inc("sV", v.reduce_max(m2, L2, axis=AX.X))
                v.wait_ge(sem["sV"], w)
                v.tensor_scalar(k2, L2, m2, None, ALU.is_equal)
                M[("Ar1", j, tt)] = inc("sV", v.tensor_sub(dd, m2, m1))
            block.vector(_)

            def _(sc, j=j, tt=tt):
                sc.wait_ge(sem["sV"], M[("Ar1", j, tt)])
                M[("Ar2", j, tt)] = inc("sA", sc.activation(
                    sqt[0][:, 49:50], sqt[0][:, 48:49], AF.Exp))
            block.scalar(_)

            def _(v, j=j, tt=tt):
                k1 = sqt[0][:, 16:24]; k2 = sqt[0][:, 40:48]
                ee = sqt[0][:, 49:50]; rr = sqt[0][:, 50:51]; w2 = sqt[0][:, 51:52]
                tw = sqt[1][:, 0:8]
                v.wait_ge(sem["sA"], M[("Ar2", j, tt)])
                w = inc("sV", v.tensor_scalar_add(rr, ee, 1.0))
                v.wait_ge(sem["sV"], w)
                w = inc("sV", v.reciprocal(rr, rr))
                v.wait_ge(sem["sV"], w)
                w = inc("sV", v.tensor_mul(w2, ee, rr))
                v.wait_ge(sem["sV"], w)
                prev = (j, tt - 1) if tt > 0 else ((j - 1, 3) if j > 0 else None)
                if prev is not None:
                    v.wait_ge(sem["sP"], M[("AtwT", prev[0], prev[1])])
                w = inc("sV", v.tensor_scalar(tw, k1, rr, None, ALU.mult))
                v.wait_ge(sem["sV"], w)
                M[("Atw", j, tt)] = inc("sV", v.scalar_tensor_tensor(
                    tw, k2, w2, tw, ALU.mult, ALU.add))
            block.vector(_)

            def _(t, j=j, tt=tt):
                t.wait_ge(sem["sV"], M[("Atw", j, tt)])
                M[("AtwT", j, tt)] = inc("sP", t.transpose(
                    pb[5][0:8, 0:128], sqt[1][:, 0:8], idf32_sb))
            block.tensor(_)

            def _(v, j=j, tt=tt):
                v.wait_ge(sem["sP"], M[("AtwT", j, tt)])
                M[("Atwc", j, tt)] = inc("sV", v.tensor_copy(
                    twt[j][0:8, tt * 128:(tt + 1) * 128], pb[5][0:8, 0:128]))
            block.vector(_)

        def _(q, j=j):
            q.wait_ge(sem["sV"], M[("Atwc", j, 3)])
            inc("dAgt", q.dma_start(out=agin_t.ap()[0:8, j * BLK:(j + 1) * BLK],
                                    in_=twt[j][0:8, :]))
        block.sync(_)

    if upto == "A":
        dbgA = {}
        for nm, shape, dt in [("t_xblk", [128, DT * BLK], F32),
                              ("t_xn32", [128, DT * BLK], F32),
                              ("t_srow", [128, BLK], F32),
                              ("t_agx", [D, TPC], BF16),
                              ("t_agt", [8, TPC], F32)]:
            dbgA[nm] = nc.dram_tensor(nm, shape, dt, kind="ExternalOutput").ap()

        def _(q):
            q.wait_ge(sem["sV"], M[("Axncp", 1)])
            q.wait_ge(sem["sV"], M[("Atwc", 1, 3)])
            q.wait_ge(sem["dAgx"], C["dAgx"])
            q.wait_ge(sem["dAgt"], C["dAgt"])
            inc("dDbg", q.dma_start(out=dbgA["t_xblk"], in_=xblk))
            inc("dDbg", q.dma_start(out=dbgA["t_xn32"], in_=xn32))
            inc("dDbg", q.dma_start(out=dbgA["t_srow"], in_=rstd_t))
            inc("dDbg", q.dma_start(out=dbgA["t_agx"], in_=agin_x.ap()))
            inc("dDbg", q.dma_start(out=dbgA["t_agt"], in_=agin_t.ap()))
            q.wait_ge(sem["dDbg"], C["dDbg"])
            q.wait_ge(sem["dQkv"], C["dQkv"])
            q.wait_ge(sem["dGate"], C["dGate"])
        block.sync(_)
        block.__exit__(None, None, None)
        return nc

    def _(q):
        q.wait_ge(sem["dAgx"], 32)
        inc("cAgx", q.collective_compute(
            "AllGather", ALU.bypass, replica_groups=[list(range(NCORE))],
            ins=[agin_x.ap().opt()], outs=[agout_x.ap().opt()]), 1)
        q.wait_ge(sem["dAgt"], 32)
        inc("cAgt", q.collective_compute(
            "AllGather", ALU.bypass, replica_groups=[list(range(NCORE))],
            ins=[agin_t.ap().opt()], outs=[agout_t.ap().opt()]), 1)
    block.gpsimd(_)

    def xn_load(q, tb):
        p = tb % 2
        r, half = tb // 2, tb % 2
        src = agout_x.ap()[r * D:(r + 1) * D, half * BLK:(half + 1) * BLK]
        inc(f"dXn{p}", q.dma_start(out=xnv[p], in_=src.rearrange("(t p) q -> p t q", p=128)))

    for tb in range(NB):
        p = tb % 2

        def _(q, tb=tb, p=p):
            if tb == 0:
                q.wait_ge(sem["cAgx"], 1)
            if tb >= 2:
                q.wait_ge(sem["sP"], M[("B1pe", tb - 2)])
            xn_load(q, tb)
            M[("B1x", tb)] = C[f"dXn{p}"]
        block.sync(_)

        def _(t, tb=tb, p=p):
            t.wait_ge(sem[f"dXn{p}"], M[("B1x", tb)])
            if tb == 0:
                t.wait_ge(sem["dQkv"], 48)
            if tb >= 1:
                t.wait_ge(sem["sA"], M[("B1evA", tb - 1)])
                t.wait_ge(sem["sV"], M[("B1evV", tb - 1)])
            xv = xnv[p]
            for h in range(NHC):
                for i in range(DT):
                    t.matmul(pb[0 + h][:, :], wqv[:, i, h * HD:(h + 1) * HD],
                             xv[:, i, :], start=(i == 0), stop=(i == DT - 1))
                for i in range(DT):
                    t.matmul(pb[2 + h][:, :], wkv[:, i, h * HD:(h + 1) * HD],
                             xv[:, i, :], start=(i == 0), stop=(i == DT - 1))
            for s4 in range(4):
                bank, half = 4 + s4 // 2, s4 % 2
                for i in range(DT):
                    ins = t.matmul(pb[bank][:, half * 256:(half + 1) * 256],
                                   xv[:, i, s4 * 128:(s4 + 1) * 128],
                                   wvv[:, i, :], start=(i == 0), stop=(i == DT - 1))
            M[("B1pe", tb)] = inc("sP", ins)
        block.tensor(_)

        def _(sc, tb=tb):
            sc.wait_ge(sem["sP"], M[("B1pe", tb)])
            if tb == 0:
                sc.wait_ge(sem["dPre"], PRE_ALL)
            for h in range(NHC):
                sc.activation(QT_sb[:, h * NTOK + tb * BLK:h * NTOK + (tb + 1) * BLK],
                              pb[0 + h][:, :], AF.Identity, bias=bq_sb[:, h:h + 1],
                              scale=1.0)
                ins = sc.activation(KT_sb[:, h * NTOK + tb * BLK:h * NTOK + (tb + 1) * BLK],
                                    pb[2 + h][:, :], AF.Identity, bias=bk_sb[:, h:h + 1],
                                    scale=1.0)
            M[("B1evA", tb)] = inc("sA", ins)
        block.scalar(_)

        def _(v, tb=tb):
            v.wait_ge(sem["sP"], M[("B1pe", tb)])
            if tb == 0:
                v.wait_ge(sem["sV"], M["bv"])
                v.wait_ge(sem["dAgt"], 32)
            for s4 in range(4):
                bank, half = 4 + s4 // 2, s4 % 2
                g = tb * 4 + s4
                ins = v.tensor_add(V_sb[:, g * 256:(g + 1) * 256],
                                   pb[bank][:, half * 256:(half + 1) * 256], bv_bc)
            M[("B1evV", tb)] = inc("sV", ins)
        block.vector(_)

    def _(q):
        q.wait_ge(sem["sP"], M[("B1pe", 14)])
        for m in range(4):
            inc("dMask", q.dma_start(out=mask_sb[m], in_=masks[m]))
    block.sync(_)

    baseP, baseA, baseV = C["sP"], C["sA"], C["sV"]
    sM, pvM, bcM, expM, maskM = {}, {}, {}, {}, {}
    rrowM, recM, recbcM, otvM = {}, {}, {}, {}
    otdM = {}
    p_, a_, v_ = baseP, baseA, baseV
    ot_use = {0: C["dOtW0"], 1: C["dOtW1"]}
    gq = 0
    g = 0
    for u in range(4):
        for qb in range(QBN):
            nk = 4 * qb + 4
            for kt in range(nk):
                sM[g] = p_ + 1
                pvM[g] = p_ + 2
                p_ += 2
                expM[g] = a_ + 1
                a_ += 1
                if kt >= 4 * qb:
                    maskM[g] = v_ + 1
                    v_ += 1
                g += 1
            bcM[gq] = p_ + 1
            p_ += 1
            rrowM[gq] = v_ + 1
            recM[gq] = v_ + 2
            recbcM[gq] = v_ + 3
            otvM[gq] = v_ + 4
            v_ += 4
            ot_use[gq % 2] += 16
            otdM[gq] = ot_use[gq % 2]
            gq += 1

    gq = 0
    g = 0
    for u in range(4):
        b, hh = u // 2, u % 2
        qoff = hh * NTOK + b * S
        for qb in range(QBN):
            nk = 4 * qb + 4
            g0, gq0 = g, gq

            def _(sc, u=u, qb=qb, nk=nk, g0=g0):
                for kt in range(nk):
                    gg = g0 + kt
                    sc.wait_ge(sem["sP"], sM[gg])
                    if gg >= 4:
                        sc.wait_ge(sem["sP"], pvM[gg - 4])
                    ins = sc.activation(pt_t[gg % 4], pb[gg % 2][:, :], AF.Exp,
                                        bias=neg8, scale=SCALE)
                    assert inc("sA", ins) == expM[gg]
            block.scalar(_)

            def _(v, u=u, qb=qb, nk=nk, g0=g0, gq0=gq0):
                if u == 0 and qb == 0:
                    v.wait_ge(sem["dMask"], 64)
                for kt in range(4 * qb, nk):
                    gg = g0 + kt
                    v.wait_ge(sem["sA"], expM[gg])
                    ins = v.tensor_mul(pt_t[gg % 4], pt_t[gg % 4], mask_sb[kt - 4 * qb])
                    assert inc("sV", ins) == maskM[gg]
                rbank = 3 if gq0 % 2 == 0 else 6
                obank = 2 if gq0 % 2 == 0 else 5
                v.wait_ge(sem["sP"], pvM[g0 + nk - 1])
                if gq0 >= 2:
                    v.wait_ge(sem["sP"], bcM[gq0 - 2])
                assert inc("sV", v.tensor_copy(rrow[gq0 % 2], pb[rbank][0:1, :])) == rrowM[gq0]
                v.wait_ge(sem["sV"], rrowM[gq0])
                assert inc("sV", v.reciprocal(rrow[gq0 % 2], rrow[gq0 % 2])) == recM[gq0]
                v.wait_ge(sem["sP"], bcM[gq0])
                assert inc("sV", v.tensor_copy(rec_bc, pb[4][:, :])) == recbcM[gq0]
                v.wait_ge(sem["sV"], recbcM[gq0])
                if gq0 >= 2:
                    v.wait_ge(sem[f"dOtW{gq0 % 2}"], otdM[gq0 - 2])
                ins = v.tensor_mul(otv[gq0 % 2], pb[obank][:, :], rec_bc)
                assert inc("sV", ins) == otvM[gq0]
            block.vector(_)

            def _(t, u=u, qb=qb, nk=nk, g0=g0, gq0=gq0, hh=hh, b=b, qoff=qoff):
                obank = 2 if gq0 % 2 == 0 else 5
                rbank = 3 if gq0 % 2 == 0 else 6
                if u == 0 and qb == 0:
                    t.wait_ge(sem["sA"], M[("B1evA", NB - 1)])
                    t.wait_ge(sem["sV"], M[("B1evV", NB - 1)])
                if gq0 >= 2:
                    t.wait_ge(sem["sV"], otvM[gq0 - 2])
                for kt in range(nk):
                    gg = g0 + kt
                    if gg >= 2:
                        t.wait_ge(sem["sA"], expM[gg - 2])
                    ins = t.matmul(pb[gg % 2][:, :],
                                   KT_sb[:, qoff + kt * HD:qoff + (kt + 1) * HD],
                                   QT_sb[:, qoff + qb * BLK:qoff + (qb + 1) * BLK],
                                   start=True, stop=True)
                    assert inc("sP", ins) == sM[gg]
                    t.wait_ge(sem["sA"], expM[gg])
                    if kt >= 4 * qb:
                        t.wait_ge(sem["sV"], maskM[gg])
                    t.matmul(pb[obank][:, :],
                             V_sb[:, (b * 32 + kt) * 256 + hh * HD:(b * 32 + kt) * 256 + (hh + 1) * HD],
                             pt_t[gg % 4], start=(kt == 0), stop=(kt == nk - 1))
                    ins = t.matmul(pb[rbank][0:1, :], ones_col_bf, pt_t[gg % 4],
                                   start=(kt == 0), stop=(kt == nk - 1))
                    assert inc("sP", ins) == pvM[gg]
                t.wait_ge(sem["sV"], recM[gq0])
                if gq0 >= 1:
                    t.wait_ge(sem["sV"], recbcM[gq0 - 1])
                ins = t.matmul(pb[4][:, :], ones_row, rrow[gq0 % 2], start=True, stop=True)
                assert inc("sP", ins) == bcM[gq0]
            block.tensor(_)

            def _(q, gq0=gq0, hh=hh, b=b, qb=qb):
                q.wait_ge(sem["sV"], otvM[gq0])
                ins = q.dma_start(
                    out=ot_d.ap()[hh * HD:(hh + 1) * HD,
                                  b * S + qb * BLK:b * S + (qb + 1) * BLK],
                    in_=otv[gq0 % 2])
                assert inc(f"dOtW{gq0 % 2}", ins) == otdM[gq0]
            block.sync(_)

            g += nk
            gq += 1

    M["B2peEnd"] = C["sP"]
    M["B2vEnd"] = C["sV"]
    OTW_ALL = (C["dOtW0"], C["dOtW1"])

    def _(q):
        q.wait_ge(sem["sP"], M["B2peEnd"])
        inc("dUp", q.dma_start(out=uv, in_=uTd.rearrange("(t p) f -> p t f", p=128)))
        inc("dDown", q.dma_start(out=dvw, in_=dTd.rearrange("(t p) d -> p t d", p=128)))
        inc("dWo", q.dma_start(out=wov, in_=woT.rearrange("(h p) d -> p h d", p=128)))
    block.gpsimd(_)

    tborder = [2 * i for i in range(8)] + [2 * i + 1 for i in range(8)]
    stg_half_mark = None
    baseP3, baseA3, baseV3 = C["sP"], C["sA"], C["sV"]
    stg_base = {0: C["dStg0"], 1: C["dStg1"]}
    for li in range(NB):
        M[("B3twpe", li)] = baseP3 + li * (1 + FT + DT) + 1
        for ft in range(FT):
            M[("B3gu", li, ft)] = baseP3 + li * (1 + FT + DT) + 1 + ft + 1
            M[("B3sg", li, ft)] = baseA3 + li * FT + ft + 1
        for d in range(DT):
            M[("B3dn", li, d)] = baseP3 + li * (1 + FT + DT) + 1 + FT + d + 1
        M[("B3twc", li)] = baseV3 + li * (1 + 2 * FT + DT) + 1
        for ft in range(FT):
            M[("B3h", li, ft)] = baseV3 + li * (1 + 2 * FT + DT) + 1 + 2 * (ft + 1)
        for d in range(DT):
            M[("B3ev", li, d)] = baseV3 + li * (1 + 2 * FT + DT) + 1 + 2 * FT + d + 1
        for d in range(DT):
            sp = d % 2
            stg_base[sp] += 16
            M[("B3st", li, d)] = stg_base[sp]
    for li, tb in enumerate(tborder):
        p = li % 2
        r, half = tb // 2, tb % 2

        def _(q, li=li, tb=tb, p=p, r=r, half=half):
            if li == 0:
                q.wait_ge(sem["sP"], M["B2peEnd"])
                q.wait_ge(sem["sV"], M["B2vEnd"])
                q.wait_ge(sem["dOtW0"], OTW_ALL[0])
                q.wait_ge(sem["dOtW1"], OTW_ALL[1])
                q.wait_ge(sem["cAgt"], 1)
            if li >= 2:
                q.wait_ge(sem["sP"], M[("B3gu", li - 2, FT - 1)])
                q.wait_ge(sem["sP"], M[("B3dn", li - 2, DT - 1)])
            src = agout_x.ap()[r * D:(r + 1) * D, half * BLK:(half + 1) * BLK]
            inc(f"dXn{p}", q.dma_start(out=xnv[p],
                                       in_=src.rearrange("(t p) q -> p t q", p=128)))
            for hh in range(NHC):
                inc(f"dXn{p}", q.dma_start(
                    out=otin[2 * p + hh],
                    in_=ot_d.ap()[hh * HD:(hh + 1) * HD, tb * BLK:(tb + 1) * BLK]))
            inc(f"dXn{p}", q.dma_start(
                out=twtile[p],
                in_=agout_t.ap()[r * 8:(r + 1) * 8, half * BLK:(half + 1) * BLK]))
            M[("B3ld", li)] = C[f"dXn{p}"]
        block.sync(_)

        def _(t, li=li, tb=tb, p=p):
            t.wait_ge(sem[f"dXn{p}"], M[("B3ld", li)])
            if li == 0:
                t.wait_ge(sem["dGate"], 16)
                t.wait_ge(sem["sV"], M["B2vEnd"])
            if li >= 1:
                t.wait_ge(sem["sV"], M[("B3twc", li - 1)])
            assert inc("sP", t.matmul(
                pb[4][:, :], selb_sb, twtile[p],
                start=True, stop=True)) == M[("B3twpe", li)]
            xv = xnv[p]
            for ft in range(FT):
                if li > 0 or ft > 0:
                    pft = (li, ft - 1) if ft > 0 else (li - 1, FT - 1)
                    t.wait_ge(sem["sA"], M[("B3sg", pft[0], pft[1])])
                    t.wait_ge(sem["sV"], M[("B3h", pft[0], pft[1])])
                for i in range(DT):
                    t.matmul(pb[0][:, :], gv[:, i, ft * HD:(ft + 1) * HD],
                             xv[:, i, :], start=(i == 0), stop=(i == DT - 1))
                if ft == 0 and li == 0:
                    t.wait_ge(sem["dUp"], 16)
                for i in range(DT):
                    ins = t.matmul(pb[1][:, :], uv[:, i, ft * HD:(ft + 1) * HD],
                                   xv[:, i, :], start=(i == 0), stop=(i == DT - 1))
                assert inc("sP", ins) == M[("B3gu", li, ft)]
            for d in range(DT):
                bank = 2 + d % 2
                if d == 0:
                    t.wait_ge(sem["sV"], M[("B3h", li, FT - 1)])
                    if li == 0:
                        t.wait_ge(sem["dDown"], 16)
                        t.wait_ge(sem["dWo"], 16)
                if li > 0 or d >= 2:
                    pd = (li, d - 2) if d >= 2 else (li - 1, DT - 2 + d)
                    t.wait_ge(sem["sV"], M[("B3ev", pd[0], pd[1])])
                for ft in range(FT):
                    t.matmul(pb[bank][:, :], dvw[:, ft, d * HD:(d + 1) * HD],
                             hp_t[ft], start=(ft == 0), stop=False)
                for hh in range(NHC):
                    ins = t.matmul(pb[bank][:, :], wov[:, hh, d * HD:(d + 1) * HD],
                                   otin[2 * p + hh],
                                   start=False, stop=(hh == NHC - 1))
                assert inc("sP", ins) == M[("B3dn", li, d)]
        block.tensor(_)

        def _(sc, li=li):
            for ft in range(FT):
                sc.wait_ge(sem["sP"], M[("B3gu", li, ft)])
                assert inc("sA", sc.activation(
                    sg_t[ft % 2], pb[0][:, :], AF.Silu)) == M[("B3sg", li, ft)]
        block.scalar(_)

        def _(v, li=li, p=p, half=half):
            v.wait_ge(sem["sP"], M[("B3twpe", li)])
            assert inc("sV", v.tensor_copy(twbc[p], pb[4][:, :])) == M[("B3twc", li)]
            for ft in range(FT):
                v.wait_ge(sem["sA"], M[("B3sg", li, ft)])
                if li >= 1 and ft == 0:
                    v.wait_ge(sem["sP"], M[("B3dn", li - 1, DT - 1)])
                w = inc("sV", v.tensor_mul(hp_t[ft], pb[1][:, :], sg_t[ft % 2]))
                v.wait_ge(sem["sV"], w)
                assert inc("sV", v.tensor_mul(
                    hp_t[ft], hp_t[ft], twbc[p])) == M[("B3h", li, ft)]
            for d in range(DT):
                bank = 2 + d % 2
                v.wait_ge(sem["sP"], M[("B3dn", li, d)])
                if li == 8 and d < 2:
                    v.wait_ge(sem["dStg0"], stg_half_mark[0])
                    v.wait_ge(sem["dStg1"], stg_half_mark[1])
                elif li > 0 or d >= 2:
                    pd = (li, d - 2) if d >= 2 else (li - 1, DT - 2 + d)
                    v.wait_ge(sem[f"dStg{d % 2}"], M[("B3st", pd[0], pd[1])])
                assert inc("sV", v.tensor_copy(stg[d % 2], pb[bank][:, :])) == M[("B3ev", li, d)]
        block.vector(_)

        def _(q, li=li, d_all=True, r=r, half=half):
            for d in range(DT):
                q.wait_ge(sem["sV"], M[("B3ev", li, d)])
                ins = q.dma_start(
                    out=part[half].ap()[r * D + d * HD:r * D + (d + 1) * HD, :],
                    in_=stg[d % 2])
                assert inc(f"dStg{d % 2}", ins) == M[("B3st", li, d)]
        block.sync(_)

        if li == 7:
            stg_half_mark = (C["dStg0"], C["dStg1"])

    def _(q):
        q.wait_ge(sem["dStg0"], stg_half_mark[0])
        q.wait_ge(sem["dStg1"], stg_half_mark[1])
        inc("cRsA", q.collective_compute(
            "ReduceScatter", ALU.add, replica_groups=[list(range(NCORE))],
            ins=[part[0].ap().opt()], outs=[rsout[0].ap().opt()]), 1)
        q.wait_ge(sem["dStg0"], C["dStg0"])
        q.wait_ge(sem["dStg1"], C["dStg1"])
        inc("cRsB", q.collective_compute(
            "ReduceScatter", ALU.add, replica_groups=[list(range(NCORE))],
            ins=[part[1].ap().opt()], outs=[rsout[1].ap().opt()]), 1)
    block.gpsimd(_)

    it = 0
    for j2 in range(2):
        for d in range(DT):
            p = it % 4

            def _(q, d=d, j2=j2, it=it, p=p):
                if it == 0:
                    q.wait_ge(sem["cRsA"], 1)
                if it == DT:
                    q.wait_ge(sem["cRsB"], 1)
                if it >= 4:
                    q.wait_ge(sem["sA"], M[("Cb", it - 4)])
                    q.wait_ge(sem[f"dCo{p}"], M[("Co", it - 4)])
                inc(f"dCl{p}", q.dma_start(
                    out=c_rs[p], in_=rsout[j2].ap()[d * HD:(d + 1) * HD, :]))
                inc(f"dCl{p}", q.dma_start(
                    out=c_x[p],
                    in_=xT[d * HD:(d + 1) * HD, j2 * BLK:(j2 + 1) * BLK]))
                M[("Cl", it)] = C[f"dCl{p}"]
            block.sync(_)

            def _(v, d=d, it=it, p=p):
                v.wait_ge(sem[f"dCl{p}"], M[("Cl", it)])
                M[("Ca", it)] = inc("sV", v.tensor_add(c_o[p], c_rs[p], c_x[p]))
            block.vector(_)

            def _(sc, d=d, it=it, p=p):
                sc.wait_ge(sem["sV"], M[("Ca", it)])
                M[("Cb", it)] = inc("sA", sc.activation(
                    c_o[p], c_o[p], AF.Identity,
                    bias=bo_sb[:, d:d + 1], scale=1.0))
            block.scalar(_)

            def _(q, d=d, j2=j2, it=it, p=p):
                q.wait_ge(sem["sA"], M[("Cb", it)])
                ins = q.dma_start(
                    out=outT[d * HD:(d + 1) * HD, j2 * BLK:(j2 + 1) * BLK],
                    in_=c_o[p])
                M[("Co", it)] = inc(f"dCo{p}", ins)
            block.sync(_)
            it += 1

    if debug:
        def _(q):
            for nm, (dst, src) in dbg.items():
                inc("dDbg", q.dma_start(out=dst, in_=src.ap()))
        block.sync(_)

    def _(q):
        for k in SEM_NAMES:
            if (k.startswith("d") or k.startswith("c")) and C[k]:
                q.wait_ge(sem[k], C[k])
    block.sync(_)

    block.__exit__(None, None, None)
    return nc


def prep_inputs(L0_attn_h, L0_attn_Wq, L0_attn_bq, L0_attn_Wk, L0_attn_bk,
                L0_attn_Wv, L0_attn_bv, L0_attn_Wo, L0_attn_bo,
                L0_ffn_gate_w, L0_ffn_gate_proj, L0_ffn_up_proj, L0_ffn_down_proj):
    x = np.ascontiguousarray(np.asarray(L0_attn_h, dtype=np.float32).reshape(NTOK, D).T)
    wqT = np.ascontiguousarray(np.asarray(L0_attn_Wq).T)
    wkT = np.ascontiguousarray(np.asarray(L0_attn_Wk).T)
    wvT = np.ascontiguousarray(np.asarray(L0_attn_Wv).T)
    woT = np.ascontiguousarray(np.asarray(L0_attn_Wo).T)
    gwT = np.ascontiguousarray(np.asarray(L0_ffn_gate_w).T)
    bq = np.asarray(L0_attn_bq); bk = np.asarray(L0_attn_bk)
    bv = np.asarray(L0_attn_bv); bo = np.asarray(L0_attn_bo)
    gp = np.asarray(L0_ffn_gate_proj); up = np.asarray(L0_ffn_up_proj)
    dp = np.asarray(L0_ffn_down_proj)

    mk = np.zeros((4, 128, BLK), dtype=BFNP)
    for m in range(4):
        i = np.arange(128)[:, None]
        jj = np.arange(BLK)[None, :]
        mk[m] = (128 * m + i <= jj).astype(BFNP)
    idf = np.eye(128, dtype=np.float32)

    in_maps = []
    for c in range(NCORE):
        hsl = slice(2 * c * HD, (2 * c + 2) * HD)
        sel = np.zeros((8, 128), np.float32)
        sel[c, :] = 1.0
        in_maps.append({
            "xT": np.ascontiguousarray(x[:, c * TPC:(c + 1) * TPC]),
            "wqT": np.ascontiguousarray(wqT[:, hsl]),
            "wkT": np.ascontiguousarray(wkT[:, hsl]),
            "wvT": np.ascontiguousarray(wvT[:, hsl]),
            "woT": np.ascontiguousarray(woT[hsl, :]),
            "bqh": np.ascontiguousarray(bq[hsl].reshape(NHC, HD).T),
            "bkh": np.ascontiguousarray(bk[hsl].reshape(NHC, HD).T),
            "bvr": np.ascontiguousarray(bv[hsl].reshape(1, NHC * HD)),
            "boh": np.ascontiguousarray(bo.reshape(DT, 128).T),
            "gwT": gwT,
            "gTd": np.ascontiguousarray(gp[c].T),
            "uTd": np.ascontiguousarray(up[c].T),
            "dTd": np.ascontiguousarray(dp[c].T),
            "selb": sel,
            "masks": mk,
            "idf32": idf,
        })
    return in_maps


_CACHE = {}


def kernel(**inputs):
    if "nc" not in _CACHE:
        _CACHE["nc"] = build_program(debug=False)
    nc = _CACHE["nc"]
    in_maps = prep_inputs(**inputs)
    res = run_bass_kernel_spmd(nc, in_maps, core_ids=list(range(NCORE)))
    shards = [r["outT"] for r in res.results]
    outT = np.concatenate(shards, axis=1)
    return np.ascontiguousarray(outT.T).reshape(2, S, D).astype(np.float32)


# version 03
# speedup vs baseline: 1.1152x; 1.0319x over previous
import sys

import numpy as np
import ml_dtypes

if "/opt/trn_rl_repo" not in sys.path:
    sys.path.insert(0, "/opt/trn_rl_repo")

import concourse.bass as bass
import concourse.mybir as mybir
from concourse.bass_utils import run_bass_kernel_spmd

F32 = mybir.dt.float32
BF16 = mybir.dt.bfloat16
AF = mybir.ActivationFunctionType
ALU = mybir.AluOpType
AX = mybir.AxisListType
BFNP = ml_dtypes.bfloat16

NCORE = 8
D = 2048
NTOK = 8192
TPC = NTOK // NCORE
S = 4096
HD = 128
NHC = 2
DT = D // 128
F = 1408
FT = F // 128
BLK = 512
NB = NTOK // BLK
QBN = S // BLK
SCALE = float(1.0 / np.sqrt(128.0))
KB = 1024

SEM_NAMES = ["sP", "sV", "sA",
             "dPre", "dX", "dAgx", "dAgt", "dMask",
             "dXn0", "dXn1", "dOtW0", "dOtW1", "dStg0", "dStg1",
             "dCl0", "dCl1", "dCl2", "dCl3", "dCo0", "dCo1", "dCo2", "dCo3", "dDbg",
             "dQkv", "dGate", "dUp", "dDown", "dWo",
             "cAgx", "cAgt", "cRsA", "cRsB"]


def build_program(debug=False, upto=None):
    nc = bass.Bass("TRN2", target_bir_lowering=False, debug=False)

    def din(name, shape, dt=F32):
        return nc.dram_tensor(name, shape, dt, kind="ExternalInput").ap()

    xT = din("xT", [D, TPC])
    wqT = din("wqT", [D, NHC * HD])
    wkT = din("wkT", [D, NHC * HD])
    wvT = din("wvT", [D, NHC * HD])
    woT = din("woT", [NHC * HD, D])
    bqh = din("bqh", [HD, NHC])
    bkh = din("bkh", [HD, NHC])
    bvr = din("bvr", [1, NHC * HD])
    boh = din("boh", [128, DT])
    gwT = din("gwT", [D, 8])
    gTd = din("gTd", [D, F])
    uTd = din("uTd", [D, F])
    dTd = din("dTd", [F, D])
    selb = din("selb", [8, 128])
    masks = din("masks", [4, 128, BLK], BF16)
    idf32 = din("idf32", [128, 128])

    outT = nc.dram_tensor("outT", [D, TPC], F32, kind="ExternalOutput").ap()

    agin_x = nc.dram_tensor("agin_x", [D, TPC], BF16)
    agout_x = nc.dram_tensor("agout_x", [NCORE * D, TPC], BF16)
    agin_t = nc.dram_tensor("agin_t", [8, TPC], F32)
    agout_t = nc.dram_tensor("agout_t", [NCORE * 8, TPC], F32)
    ot_d = nc.dram_tensor("ot_d", [NHC * HD, NTOK], BF16)
    part = [nc.dram_tensor(f"part{h}", [NCORE * D, BLK], F32) for h in range(2)]
    rsout = [nc.dram_tensor(f"rsout{h}", [D, BLK], F32) for h in range(2)]

    dbg = {}
    if debug:
        for nm, t in [("d_agx", agout_x), ("d_agt", agout_t), ("d_ot", ot_d),
                      ("d_part0", part[0]), ("d_part1", part[1]),
                      ("d_rs0", rsout[0]), ("d_rs1", rsout[1])]:
            dbg[nm] = (nc.dram_tensor(nm, list(t.ap().shape), t.ap().dtype,
                                      kind="ExternalOutput").ap(), t)

    ARENA = 205 * KB
    arena = nc.alloc_sbuf_tensor("arena", [128, ARENA], mybir.dt.uint8)

    def cv(off, nelem, dt, parts=128):
        nb = {F32: 4, BF16: 2}[dt]
        assert off % 64 == 0 and off + nelem * nb <= ARENA, f"bad carve {off}"
        return arena.ap()[0:parts, off:off + nelem * nb].bitcast(dt)

    o = 0
    onesm = cv(o, 128, F32); o += 512
    ones_col_bf = cv(o, 1, BF16); o += 64
    ones_row = cv(o, 128, F32, parts=1); o += 512
    neg8 = cv(o, 1, F32); o += 64
    eps128 = cv(o, 1, F32); o += 64
    bq_sb = cv(o, 2, F32); o += 64
    bk_sb = cv(o, 2, F32); o += 64
    bo_sb = cv(o, DT, F32); o += 64
    bv_bc = cv(o, NHC * HD, BF16); o += 512
    bvr_sb = cv(o, NHC * HD, F32, parts=1); o += 1024
    idf32_sb = cv(o, 128, F32); o += 512
    gwT_sb = cv(o, DT * 8, F32); o += 512
    selb_sb = cv(o, 128, F32, parts=8); o += 512
    assert o <= 6 * KB

    xnb = [cv(6 * KB + i * 16 * KB, DT * BLK, BF16) for i in range(2)]
    pt_t = [cv(6 * KB + i * KB, BLK, BF16) for i in range(4)]
    mask_sb = [cv(10 * KB + i * KB, BLK, BF16) for i in range(4)]
    otv = [cv(14 * KB + i * KB, BLK, BF16) for i in range(2)]
    rec_bc = cv(16 * KB, BLK, BF16)
    rrow = [cv(17 * KB + 2048 * i, BLK, F32, parts=1) for i in range(2)]

    QT_sb = cv(38 * KB, NHC * NTOK, BF16)
    KT_sb = cv(70 * KB, NHC * NTOK, BF16)
    V_sb = cv(102 * KB, 64 * NHC * HD, BF16)
    xblk = cv(38 * KB, DT * BLK, F32)
    xn32 = cv(70 * KB, DT * BLK, F32)
    sqt = [cv(102 * KB + i * 2 * KB, BLK, F32) for i in range(2)]
    rstd_t = cv(106 * KB, BLK, F32)
    twt = [cv(108 * KB + i * 2 * KB, BLK, F32, parts=8) for i in range(2)]
    uT_sb = cv(38 * KB, DT * F, BF16)
    dT_sb = cv(82 * KB, FT * D, BF16)
    wo_sb = cv(126 * KB, NHC * D, BF16)
    c_rs = [cv(38 * KB + i * 2 * KB, BLK, F32) for i in range(4)]
    c_x = [cv(46 * KB + i * 2 * KB, BLK, F32) for i in range(4)]
    c_o = [cv(54 * KB + i * 2 * KB, BLK, F32) for i in range(4)]

    gT_sb = cv(134 * KB, DT * F, BF16)

    wq_sb = cv(178 * KB, DT * NHC * HD, BF16)
    wk_sb = cv(186 * KB, DT * NHC * HD, BF16)
    wv_sb = cv(194 * KB, DT * NHC * HD, BF16)
    hp_t = [cv(178 * KB + i * KB, BLK, BF16) for i in range(FT)]
    otin = [cv(189 * KB + i * KB, BLK, BF16) for i in range(4)]
    stg = [cv(193 * KB + i * 2 * KB, BLK, F32) for i in range(2)]
    sg_t = [cv(197 * KB + i * KB, BLK, BF16) for i in range(2)]
    twbc = [cv(199 * KB + i * KB, BLK, BF16) for i in range(2)]
    twtile = [cv(201 * KB + i * 2 * KB, BLK, F32, parts=8) for i in range(2)]

    pb = [nc.alloc_psum_tensor(f"pb{i}", [128, BLK], F32) for i in range(8)]

    C = {k: 0 for k in SEM_NAMES}
    sem = {k: nc.semaphore(k).__enter__() for k in SEM_NAMES}

    def inc(name, ins, n=None):
        if n is None:
            n = 1 if name in ("sP", "sV", "sA") or name.startswith("c") else 16
        ins.then_inc(sem[name], n)
        C[name] += n
        return C[name]

    M = {}
    block = nc.Block().__enter__()

    wqv = wq_sb.rearrange("p (t h) -> p t h", t=DT)
    wkv = wk_sb.rearrange("p (t h) -> p t h", t=DT)
    wvv = wv_sb.rearrange("p (t h) -> p t h", t=DT)
    gwv = gwT_sb.rearrange("p (t e) -> p t e", t=DT)
    gv = gT_sb.rearrange("p (t f) -> p t f", t=DT)
    uv = uT_sb.rearrange("p (t f) -> p t f", t=DT)
    dvw = dT_sb.rearrange("p (t d) -> p t d", t=FT)
    wov = wo_sb.rearrange("p (h d) -> p h d", h=NHC)
    xnv = [bb.rearrange("p (t q) -> p t q", t=DT) for bb in xnb]

    def _(q):
        inc("dPre", q.dma_start(out=idf32_sb, in_=idf32))
        inc("dPre", q.dma_start(out=bq_sb, in_=bqh))
        inc("dPre", q.dma_start(out=bk_sb, in_=bkh))
        inc("dPre", q.dma_start(out=bo_sb, in_=boh))
        inc("dPre", q.dma_start(out=bvr_sb, in_=bvr))
        inc("dPre", q.dma_start(out=gwv, in_=gwT.rearrange("(t p) e -> p t e", p=128)))
        inc("dPre", q.dma_start(out=selb_sb, in_=selb))
    block.sync(_)
    PRE_ALL = C["dPre"]

    def _(q):
        inc("dQkv", q.dma_start(out=wqv, in_=wqT.rearrange("(t p) h -> p t h", p=128)))
        inc("dQkv", q.dma_start(out=wkv, in_=wkT.rearrange("(t p) h -> p t h", p=128)))
        inc("dQkv", q.dma_start(out=wvv, in_=wvT.rearrange("(t p) h -> p t h", p=128)))
        inc("dGate", q.dma_start(out=gv, in_=gTd.rearrange("(t p) f -> p t f", p=128)))
    block.gpsimd(_)

    def _(v):
        v.memset(onesm, 1.0)
        v.memset(ones_col_bf, 1.0)
        v.memset(ones_row, 1.0)
        v.memset(neg8, -8.0)
        M["consts"] = inc("sV", v.memset(eps128, 1e-5))
    block.vector(_)

    def _(t):
        t.wait_ge(sem["dPre"], PRE_ALL)
        t.wait_ge(sem["sV"], M["consts"])
        M["bv_pe"] = inc("sP", t.matmul(pb[7][:, 0:NHC * HD], ones_row, bvr_sb,
                                        start=True, stop=True))
    block.tensor(_)

    def _(v):
        v.wait_ge(sem["sP"], M["bv_pe"])
        M["bv"] = inc("sV", v.tensor_copy(bv_bc, pb[7][:, 0:NHC * HD]))
    block.vector(_)

    for j in range(2):
        def _(q, j=j):
            if j > 0:
                q.wait_ge(sem["sP"], M[("Arpe", j - 1, 3)])
            inc("dX", q.dma_start(
                out=xblk.rearrange("p (t q) -> p t q", t=DT),
                in_=xT.rearrange("(t p) q -> p t q", p=128)[:, :, j * BLK:(j + 1) * BLK]))
        block.sync(_)

        def _(t, j=j):
            t.wait_ge(sem["dX"], (j + 1) * 16)
            t.wait_ge(sem["sV"], M["consts"])
            if j > 0:
                t.wait_ge(sem["sV"], M[("Axn1", j - 1)])
            for i in range(DT):
                ins = t.matmul(pb[0][:, :], onesm,
                               xblk[:, i * BLK:(i + 1) * BLK],
                               start=(i == 0), stop=(i == DT - 1))
            M[("Asum", j)] = inc("sP", ins)
        block.tensor(_)

        for i in range(DT):
            def _(sc, j=j, i=i):
                if i == 0:
                    sc.wait_ge(sem["dX"], (j + 1) * 16)
                    if j > 0:
                        sc.wait_ge(sem["sV"], M[("Atw", j - 1, 3)])
                        sc.wait_ge(sem["sP"], M[("AtwT", j - 1, 3)])
                if i >= 2:
                    sc.wait_ge(sem["sP"], M[("Asqpe", j, i - 2)])
                M[("Asq", j, i)] = inc("sA", sc.activation(
                    sqt[i % 2], xblk[:, i * BLK:(i + 1) * BLK], AF.Square))
            block.scalar(_)

            def _(t, j=j, i=i):
                t.wait_ge(sem["sA"], M[("Asq", j, i)])
                M[("Asqpe", j, i)] = inc("sP", t.matmul(
                    pb[1][:, :], onesm, sqt[i % 2],
                    start=(i == 0), stop=(i == DT - 1)))
            block.tensor(_)

        def _(v, j=j):
            v.wait_ge(sem["sP"], M[("Asum", j)])
            v.wait_ge(sem["sP"], M[("Asqpe", j, DT - 1)])
            w = inc("sV", v.tensor_scalar_mul(sqt[0], pb[0][:, :], 1.0 / D))
            w = inc("sV", v.tensor_scalar_mul(sqt[1], pb[1][:, :], 1.0 / D))
            v.wait_ge(sem["sV"], w)
            w = inc("sV", v.tensor_mul(sqt[0], sqt[0], sqt[0]))
            v.wait_ge(sem["sV"], w)
            M[("Avar", j)] = inc("sV", v.tensor_sub(sqt[1], sqt[1], sqt[0]))
        block.vector(_)

        def _(sc, j=j):
            sc.wait_ge(sem["sV"], M[("Avar", j)])
            M[("Astd", j)] = inc("sA", sc.activation(rstd_t, sqt[1], AF.Sqrt,
                                                     bias=eps128, scale=1.0))
        block.scalar(_)

        def _(v, j=j):
            v.wait_ge(sem["sA"], M[("Astd", j)])
            M[("Arstd", j)] = inc("sV", v.reciprocal(rstd_t, rstd_t))
            v.wait_ge(sem["sV"], M[("Arstd", j)])
            for i in range(DT):
                ins = v.scalar_tensor_tensor(xn32[:, i * BLK:(i + 1) * BLK],
                                             pb[0][:, :], -1.0 / D,
                                             xblk[:, i * BLK:(i + 1) * BLK],
                                             ALU.mult, ALU.add)
            w = inc("sV", ins)
            v.wait_ge(sem["sV"], w)
            for i in range(DT):
                ins = v.tensor_mul(xn32[:, i * BLK:(i + 1) * BLK],
                                   xn32[:, i * BLK:(i + 1) * BLK], rstd_t)
            M[("Axn1", j)] = inc("sV", ins)
            v.wait_ge(sem["sV"], M[("Axn1", j)])
            for i in range(DT):
                ins = v.tensor_copy(xnb[j][:, i * BLK:(i + 1) * BLK],
                                    xn32[:, i * BLK:(i + 1) * BLK])
            M[("Axncp", j)] = inc("sV", ins)
        block.vector(_)

        def _(q, j=j):
            q.wait_ge(sem["sV"], M[("Axncp", j)])
            inc("dAgx", q.dma_start(
                out=agin_x.ap().rearrange("(t p) q -> p t q", p=128)[:, :, j * BLK:(j + 1) * BLK],
                in_=xnv[j]))
        block.sync(_)

        for tt in range(4):
            def _(t, j=j, tt=tt):
                if tt == 0:
                    t.wait_ge(sem["dPre"], PRE_ALL)
                    t.wait_ge(sem["sV"], M[("Axn1", j)])
                prev = (j, tt - 1) if tt > 0 else ((j - 1, 3) if j > 0 else None)
                if prev is not None:
                    t.wait_ge(sem["sV"], M[("ArL", prev[0], prev[1])])
                for i in range(DT):
                    ins = t.matmul(pb[4][0:128, 0:8],
                                   xn32[:, i * BLK + tt * 128:i * BLK + tt * 128 + 128],
                                   gwv[:, i, :], start=(i == 0), stop=(i == DT - 1))
                M[("Arpe", j, tt)] = inc("sP", ins)
            block.tensor(_)

            def _(v, j=j, tt=tt):
                L = sqt[0][:, 0:8]; m1 = sqt[0][:, 8:9]; k1 = sqt[0][:, 16:24]
                L2 = sqt[0][:, 24:32]; m2 = sqt[0][:, 32:33]; k2 = sqt[0][:, 40:48]
                dd = sqt[0][:, 48:49]
                v.wait_ge(sem["sP"], M[("Arpe", j, tt)])
                M[("ArL", j, tt)] = w = inc("sV", v.tensor_copy(L, pb[4][0:128, 0:8]))
                v.wait_ge(sem["sV"], w)
                w = inc("sV", v.reduce_max(m1, L, axis=AX.X))
                v.wait_ge(sem["sV"], w)
                w = inc("sV", v.tensor_scalar(k1, L, m1, None, ALU.is_equal))
                v.wait_ge(sem["sV"], w)
                w = inc("sV", v.scalar_tensor_tensor(L2, k1, -1e30, L, ALU.mult, ALU.add))
                v.wait_ge(sem["sV"], w)
                w = inc("sV", v.reduce_max(m2, L2, axis=AX.X))
                v.wait_ge(sem["sV"], w)
                v.tensor_scalar(k2, L2, m2, None, ALU.is_equal)
                M[("Ar1", j, tt)] = inc("sV", v.tensor_sub(dd, m2, m1))
            block.vector(_)

            def _(sc, j=j, tt=tt):
                sc.wait_ge(sem["sV"], M[("Ar1", j, tt)])
                M[("Ar2", j, tt)] = inc("sA", sc.activation(
                    sqt[0][:, 49:50], sqt[0][:, 48:49], AF.Exp))
            block.scalar(_)

            def _(v, j=j, tt=tt):
                k1 = sqt[0][:, 16:24]; k2 = sqt[0][:, 40:48]
                ee = sqt[0][:, 49:50]; rr = sqt[0][:, 50:51]; w2 = sqt[0][:, 51:52]
                tw = sqt[1][:, 0:8]
                v.wait_ge(sem["sA"], M[("Ar2", j, tt)])
                w = inc("sV", v.tensor_scalar_add(rr, ee, 1.0))
                v.wait_ge(sem["sV"], w)
                w = inc("sV", v.reciprocal(rr, rr))
                v.wait_ge(sem["sV"], w)
                w = inc("sV", v.tensor_mul(w2, ee, rr))
                v.wait_ge(sem["sV"], w)
                prev = (j, tt - 1) if tt > 0 else ((j - 1, 3) if j > 0 else None)
                if prev is not None:
                    v.wait_ge(sem["sP"], M[("AtwT", prev[0], prev[1])])
                w = inc("sV", v.tensor_scalar(tw, k1, rr, None, ALU.mult))
                v.wait_ge(sem["sV"], w)
                M[("Atw", j, tt)] = inc("sV", v.scalar_tensor_tensor(
                    tw, k2, w2, tw, ALU.mult, ALU.add))
            block.vector(_)

            def _(t, j=j, tt=tt):
                t.wait_ge(sem["sV"], M[("Atw", j, tt)])
                M[("AtwT", j, tt)] = inc("sP", t.transpose(
                    pb[5][0:8, 0:128], sqt[1][:, 0:8], idf32_sb))
            block.tensor(_)

            def _(v, j=j, tt=tt):
                v.wait_ge(sem["sP"], M[("AtwT", j, tt)])
                M[("Atwc", j, tt)] = inc("sV", v.tensor_copy(
                    twt[j][0:8, tt * 128:(tt + 1) * 128], pb[5][0:8, 0:128]))
            block.vector(_)

        def _(q, j=j):
            q.wait_ge(sem["sV"], M[("Atwc", j, 3)])
            inc("dAgt", q.dma_start(out=agin_t.ap()[0:8, j * BLK:(j + 1) * BLK],
                                    in_=twt[j][0:8, :]))
        block.sync(_)

    if upto == "A":
        dbgA = {}
        for nm, shape, dt in [("t_xblk", [128, DT * BLK], F32),
                              ("t_xn32", [128, DT * BLK], F32),
                              ("t_srow", [128, BLK], F32),
                              ("t_agx", [D, TPC], BF16),
                              ("t_agt", [8, TPC], F32)]:
            dbgA[nm] = nc.dram_tensor(nm, shape, dt, kind="ExternalOutput").ap()

        def _(q):
            q.wait_ge(sem["sV"], M[("Axncp", 1)])
            q.wait_ge(sem["sV"], M[("Atwc", 1, 3)])
            q.wait_ge(sem["dAgx"], C["dAgx"])
            q.wait_ge(sem["dAgt"], C["dAgt"])
            inc("dDbg", q.dma_start(out=dbgA["t_xblk"], in_=xblk))
            inc("dDbg", q.dma_start(out=dbgA["t_xn32"], in_=xn32))
            inc("dDbg", q.dma_start(out=dbgA["t_srow"], in_=rstd_t))
            inc("dDbg", q.dma_start(out=dbgA["t_agx"], in_=agin_x.ap()))
            inc("dDbg", q.dma_start(out=dbgA["t_agt"], in_=agin_t.ap()))
            q.wait_ge(sem["dDbg"], C["dDbg"])
            q.wait_ge(sem["dQkv"], C["dQkv"])
            q.wait_ge(sem["dGate"], C["dGate"])
        block.sync(_)
        block.__exit__(None, None, None)
        return nc

    def _(q):
        q.wait_ge(sem["dAgx"], 32)
        inc("cAgx", q.collective_compute(
            "AllGather", ALU.bypass, replica_groups=[list(range(NCORE))],
            ins=[agin_x.ap().opt()], outs=[agout_x.ap().opt()]), 1)
        q.wait_ge(sem["dAgt"], 32)
        inc("cAgt", q.collective_compute(
            "AllGather", ALU.bypass, replica_groups=[list(range(NCORE))],
            ins=[agin_t.ap().opt()], outs=[agout_t.ap().opt()]), 1)
    block.gpsimd(_)

    def xn_load(q, tb):
        p = tb % 2
        r, half = tb // 2, tb % 2
        src = agout_x.ap()[r * D:(r + 1) * D, half * BLK:(half + 1) * BLK]
        inc(f"dXn{p}", q.dma_start(out=xnv[p], in_=src.rearrange("(t p) q -> p t q", p=128)))

    for tb in range(NB):
        p = tb % 2

        def _(q, tb=tb, p=p):
            if tb == 0:
                q.wait_ge(sem["cAgx"], 1)
            if tb >= 2:
                q.wait_ge(sem["sP"], M[("B1pe", tb - 2)])
            xn_load(q, tb)
            M[("B1x", tb)] = C[f"dXn{p}"]
        block.sync(_)

        def _(t, tb=tb, p=p):
            t.wait_ge(sem[f"dXn{p}"], M[("B1x", tb)])
            if tb == 0:
                t.wait_ge(sem["dQkv"], 48)
            if tb >= 1:
                t.wait_ge(sem["sA"], M[("B1evA", tb - 1)])
                t.wait_ge(sem["sV"], M[("B1evV", tb - 1)])
            xv = xnv[p]
            for h in range(NHC):
                for i in range(DT):
                    ins = t.matmul(pb[0 + h][:, :], wqv[:, i, h * HD:(h + 1) * HD],
                                   xv[:, i, :], start=(i == 0), stop=(i == DT - 1))
                M[("B1q", tb, h)] = inc("sP", ins)
                for i in range(DT):
                    ins = t.matmul(pb[2 + h][:, :], wkv[:, i, h * HD:(h + 1) * HD],
                                   xv[:, i, :], start=(i == 0), stop=(i == DT - 1))
                M[("B1k", tb, h)] = inc("sP", ins)
            for s4 in range(4):
                bank, half = 4 + s4 // 2, s4 % 2
                for i in range(DT):
                    ins = t.matmul(pb[bank][:, half * 256:(half + 1) * 256],
                                   xv[:, i, s4 * 128:(s4 + 1) * 128],
                                   wvv[:, i, :], start=(i == 0), stop=(i == DT - 1))
                if s4 % 2 == 1:
                    M[("B1v", tb, s4 // 2)] = inc("sP", ins)
            M[("B1pe", tb)] = C["sP"]
        block.tensor(_)

        def _(sc, tb=tb):
            if tb == 0:
                sc.wait_ge(sem["dPre"], PRE_ALL)
            for h in range(NHC):
                sc.wait_ge(sem["sP"], M[("B1q", tb, h)])
                sc.activation(QT_sb[:, h * NTOK + tb * BLK:h * NTOK + (tb + 1) * BLK],
                              pb[0 + h][:, :], AF.Identity, bias=bq_sb[:, h:h + 1],
                              scale=1.0)
                sc.wait_ge(sem["sP"], M[("B1k", tb, h)])
                ins = sc.activation(KT_sb[:, h * NTOK + tb * BLK:h * NTOK + (tb + 1) * BLK],
                                    pb[2 + h][:, :], AF.Identity, bias=bk_sb[:, h:h + 1],
                                    scale=1.0)
            M[("B1evA", tb)] = inc("sA", ins)
        block.scalar(_)

        def _(v, tb=tb):
            if tb == 0:
                v.wait_ge(sem["sV"], M["bv"])
                v.wait_ge(sem["dAgt"], 32)
            for s4 in range(4):
                bank, half = 4 + s4 // 2, s4 % 2
                g = tb * 4 + s4
                v.wait_ge(sem["sP"], M[("B1v", tb, s4 // 2)])
                ins = v.tensor_add(V_sb[:, g * 256:(g + 1) * 256],
                                   pb[bank][:, half * 256:(half + 1) * 256], bv_bc)
            M[("B1evV", tb)] = inc("sV", ins)
        block.vector(_)

    def _(q):
        q.wait_ge(sem["sP"], M[("B1pe", 14)])
        for m in range(4):
            inc("dMask", q.dma_start(out=mask_sb[m], in_=masks[m]))
    block.sync(_)

    baseP, baseA, baseV = C["sP"], C["sA"], C["sV"]
    sM, pvM, bcM, expM, maskM = {}, {}, {}, {}, {}
    rrowM, recM, recbcM, otvM = {}, {}, {}, {}
    otdM = {}
    p_, a_, v_ = baseP, baseA, baseV
    ot_use = {0: C["dOtW0"], 1: C["dOtW1"]}
    gq = 0
    g = 0
    for u in range(4):
        for qb in range(QBN):
            nk = 4 * qb + 4
            for kt in range(nk):
                sM[g] = p_ + 1
                pvM[g] = p_ + 2
                p_ += 2
                expM[g] = a_ + 1
                a_ += 1
                if kt >= 4 * qb:
                    maskM[g] = v_ + 1
                    v_ += 1
                g += 1
            bcM[gq] = p_ + 1
            p_ += 1
            rrowM[gq] = v_ + 1
            recM[gq] = v_ + 2
            recbcM[gq] = v_ + 3
            otvM[gq] = v_ + 4
            v_ += 4
            ot_use[gq % 2] += 16
            otdM[gq] = ot_use[gq % 2]
            gq += 1

    gq = 0
    g = 0
    for u in range(4):
        b, hh = u // 2, u % 2
        qoff = hh * NTOK + b * S
        for qb in range(QBN):
            nk = 4 * qb + 4
            g0, gq0 = g, gq

            def _(sc, u=u, qb=qb, nk=nk, g0=g0):
                for kt in range(nk):
                    gg = g0 + kt
                    sc.wait_ge(sem["sP"], sM[gg])
                    if gg >= 4:
                        sc.wait_ge(sem["sP"], pvM[gg - 4])
                    ins = sc.activation(pt_t[gg % 4], pb[gg % 2][:, :], AF.Exp,
                                        bias=neg8, scale=SCALE)
                    assert inc("sA", ins) == expM[gg]
            block.scalar(_)

            def _(v, u=u, qb=qb, nk=nk, g0=g0, gq0=gq0):
                if u == 0 and qb == 0:
                    v.wait_ge(sem["dMask"], 64)
                for kt in range(4 * qb, nk):
                    gg = g0 + kt
                    v.wait_ge(sem["sA"], expM[gg])
                    ins = v.tensor_mul(pt_t[gg % 4], pt_t[gg % 4], mask_sb[kt - 4 * qb])
                    assert inc("sV", ins) == maskM[gg]
                rbank = 3 if gq0 % 2 == 0 else 6
                obank = 2 if gq0 % 2 == 0 else 5
                v.wait_ge(sem["sP"], pvM[g0 + nk - 1])
                if gq0 >= 2:
                    v.wait_ge(sem["sP"], bcM[gq0 - 2])
                assert inc("sV", v.tensor_copy(rrow[gq0 % 2], pb[rbank][0:1, :])) == rrowM[gq0]
                v.wait_ge(sem["sV"], rrowM[gq0])
                assert inc("sV", v.reciprocal(rrow[gq0 % 2], rrow[gq0 % 2])) == recM[gq0]
                v.wait_ge(sem["sP"], bcM[gq0])
                assert inc("sV", v.tensor_copy(rec_bc, pb[4][:, :])) == recbcM[gq0]
                v.wait_ge(sem["sV"], recbcM[gq0])
                if gq0 >= 2:
                    v.wait_ge(sem[f"dOtW{gq0 % 2}"], otdM[gq0 - 2])
                ins = v.tensor_mul(otv[gq0 % 2], pb[obank][:, :], rec_bc)
                assert inc("sV", ins) == otvM[gq0]
            block.vector(_)

            def _(t, u=u, qb=qb, nk=nk, g0=g0, gq0=gq0, hh=hh, b=b, qoff=qoff):
                obank = 2 if gq0 % 2 == 0 else 5
                rbank = 3 if gq0 % 2 == 0 else 6
                if u == 0 and qb == 0:
                    t.wait_ge(sem["sA"], M[("B1evA", NB - 1)])
                    t.wait_ge(sem["sV"], M[("B1evV", NB - 1)])
                if gq0 >= 2:
                    t.wait_ge(sem["sV"], otvM[gq0 - 2])
                for kt in range(nk):
                    gg = g0 + kt
                    if gg >= 2:
                        t.wait_ge(sem["sA"], expM[gg - 2])
                    ins = t.matmul(pb[gg % 2][:, :],
                                   KT_sb[:, qoff + kt * HD:qoff + (kt + 1) * HD],
                                   QT_sb[:, qoff + qb * BLK:qoff + (qb + 1) * BLK],
                                   start=True, stop=True)
                    assert inc("sP", ins) == sM[gg]
                    t.wait_ge(sem["sA"], expM[gg])
                    if kt >= 4 * qb:
                        t.wait_ge(sem["sV"], maskM[gg])
                    t.matmul(pb[obank][:, :],
                             V_sb[:, (b * 32 + kt) * 256 + hh * HD:(b * 32 + kt) * 256 + (hh + 1) * HD],
                             pt_t[gg % 4], start=(kt == 0), stop=(kt == nk - 1))
                    ins = t.matmul(pb[rbank][0:1, :], ones_col_bf, pt_t[gg % 4],
                                   start=(kt == 0), stop=(kt == nk - 1))
                    assert inc("sP", ins) == pvM[gg]
                t.wait_ge(sem["sV"], recM[gq0])
                if gq0 >= 1:
                    t.wait_ge(sem["sV"], recbcM[gq0 - 1])
                ins = t.matmul(pb[4][:, :], ones_row, rrow[gq0 % 2], start=True, stop=True)
                assert inc("sP", ins) == bcM[gq0]
            block.tensor(_)

            def _(q, gq0=gq0, hh=hh, b=b, qb=qb):
                q.wait_ge(sem["sV"], otvM[gq0])
                ins = q.dma_start(
                    out=ot_d.ap()[hh * HD:(hh + 1) * HD,
                                  b * S + qb * BLK:b * S + (qb + 1) * BLK],
                    in_=otv[gq0 % 2])
                assert inc(f"dOtW{gq0 % 2}", ins) == otdM[gq0]
            block.sync(_)

            g += nk
            gq += 1

    M["B2peEnd"] = C["sP"]
    M["B2vEnd"] = C["sV"]
    OTW_ALL = (C["dOtW0"], C["dOtW1"])

    def _(q):
        q.wait_ge(sem["sP"], M["B2peEnd"])
        inc("dUp", q.dma_start(out=uv, in_=uTd.rearrange("(t p) f -> p t f", p=128)))
        inc("dDown", q.dma_start(out=dvw, in_=dTd.rearrange("(t p) d -> p t d", p=128)))
        inc("dWo", q.dma_start(out=wov, in_=woT.rearrange("(h p) d -> p h d", p=128)))
    block.gpsimd(_)

    tborder = [2 * i for i in range(8)] + [2 * i + 1 for i in range(8)]
    stg_half_mark = None
    baseP3, baseA3, baseV3 = C["sP"], C["sA"], C["sV"]
    stg_base = {0: C["dStg0"], 1: C["dStg1"]}
    for li in range(NB):
        M[("B3twpe", li)] = baseP3 + li * (1 + FT + DT) + 1
        for ft in range(FT):
            M[("B3gu", li, ft)] = baseP3 + li * (1 + FT + DT) + 1 + ft + 1
            M[("B3sg", li, ft)] = baseA3 + li * FT + ft + 1
        for d in range(DT):
            M[("B3dn", li, d)] = baseP3 + li * (1 + FT + DT) + 1 + FT + d + 1
        M[("B3twc", li)] = baseV3 + li * (1 + 2 * FT + DT) + 1
        for ft in range(FT):
            M[("B3h", li, ft)] = baseV3 + li * (1 + 2 * FT + DT) + 1 + 2 * (ft + 1)
        for d in range(DT):
            M[("B3ev", li, d)] = baseV3 + li * (1 + 2 * FT + DT) + 1 + 2 * FT + d + 1
        for d in range(DT):
            sp = d % 2
            stg_base[sp] += 16
            M[("B3st", li, d)] = stg_base[sp]
    for li, tb in enumerate(tborder):
        p = li % 2
        r, half = tb // 2, tb % 2

        def _(q, li=li, tb=tb, p=p, r=r, half=half):
            if li == 0:
                q.wait_ge(sem["sP"], M["B2peEnd"])
                q.wait_ge(sem["sV"], M["B2vEnd"])
                q.wait_ge(sem["dOtW0"], OTW_ALL[0])
                q.wait_ge(sem["dOtW1"], OTW_ALL[1])
                q.wait_ge(sem["cAgt"], 1)
            if li >= 2:
                q.wait_ge(sem["sP"], M[("B3gu", li - 2, FT - 1)])
                q.wait_ge(sem["sP"], M[("B3dn", li - 2, DT - 1)])
            src = agout_x.ap()[r * D:(r + 1) * D, half * BLK:(half + 1) * BLK]
            inc(f"dXn{p}", q.dma_start(out=xnv[p],
                                       in_=src.rearrange("(t p) q -> p t q", p=128)))
            for hh in range(NHC):
                inc(f"dXn{p}", q.dma_start(
                    out=otin[2 * p + hh],
                    in_=ot_d.ap()[hh * HD:(hh + 1) * HD, tb * BLK:(tb + 1) * BLK]))
            inc(f"dXn{p}", q.dma_start(
                out=twtile[p],
                in_=agout_t.ap()[r * 8:(r + 1) * 8, half * BLK:(half + 1) * BLK]))
            M[("B3ld", li)] = C[f"dXn{p}"]
        block.sync(_)

        def _(t, li=li, tb=tb, p=p):
            t.wait_ge(sem[f"dXn{p}"], M[("B3ld", li)])
            if li == 0:
                t.wait_ge(sem["dGate"], 16)
                t.wait_ge(sem["sV"], M["B2vEnd"])
            if li >= 1:
                t.wait_ge(sem["sV"], M[("B3twc", li - 1)])
            assert inc("sP", t.matmul(
                pb[4][:, :], selb_sb, twtile[p],
                start=True, stop=True)) == M[("B3twpe", li)]
            xv = xnv[p]
            for ft in range(FT):
                gb, ub = 0 + ft % 2, 2 + ft % 2
                if li > 0 or ft >= 2:
                    pft = (li, ft - 2) if ft >= 2 else (li - 1, FT - 2 + ft)
                    t.wait_ge(sem["sA"], M[("B3sg", pft[0], pft[1])])
                    t.wait_ge(sem["sV"], M[("B3h", pft[0], pft[1])])
                for i in range(DT):
                    t.matmul(pb[gb][:, :], gv[:, i, ft * HD:(ft + 1) * HD],
                             xv[:, i, :], start=(i == 0), stop=(i == DT - 1))
                if ft == 0 and li == 0:
                    t.wait_ge(sem["dUp"], 16)
                for i in range(DT):
                    ins = t.matmul(pb[ub][:, :], uv[:, i, ft * HD:(ft + 1) * HD],
                                   xv[:, i, :], start=(i == 0), stop=(i == DT - 1))
                assert inc("sP", ins) == M[("B3gu", li, ft)]
            for d in range(DT):
                bank = 5 + d % 2
                if d == 0:
                    t.wait_ge(sem["sV"], M[("B3h", li, FT - 1)])
                    if li == 0:
                        t.wait_ge(sem["dDown"], 16)
                        t.wait_ge(sem["dWo"], 16)
                if li > 0 or d >= 2:
                    pd = (li, d - 2) if d >= 2 else (li - 1, DT - 2 + d)
                    t.wait_ge(sem["sV"], M[("B3ev", pd[0], pd[1])])
                for ft in range(FT):
                    t.matmul(pb[bank][:, :], dvw[:, ft, d * HD:(d + 1) * HD],
                             hp_t[ft], start=(ft == 0), stop=False)
                for hh in range(NHC):
                    ins = t.matmul(pb[bank][:, :], wov[:, hh, d * HD:(d + 1) * HD],
                                   otin[2 * p + hh],
                                   start=False, stop=(hh == NHC - 1))
                assert inc("sP", ins) == M[("B3dn", li, d)]
        block.tensor(_)

        def _(sc, li=li):
            for ft in range(FT):
                sc.wait_ge(sem["sP"], M[("B3gu", li, ft)])
                assert inc("sA", sc.activation(
                    sg_t[ft % 2], pb[0 + ft % 2][:, :], AF.Silu)) == M[("B3sg", li, ft)]
        block.scalar(_)

        def _(v, li=li, p=p, half=half):
            v.wait_ge(sem["sP"], M[("B3twpe", li)])
            assert inc("sV", v.tensor_copy(twbc[p], pb[4][:, :])) == M[("B3twc", li)]
            for ft in range(FT):
                v.wait_ge(sem["sA"], M[("B3sg", li, ft)])
                if li >= 1 and ft == 0:
                    v.wait_ge(sem["sP"], M[("B3dn", li - 1, DT - 1)])
                w = inc("sV", v.tensor_mul(hp_t[ft], pb[2 + ft % 2][:, :], sg_t[ft % 2]))
                v.wait_ge(sem["sV"], w)
                assert inc("sV", v.tensor_mul(
                    hp_t[ft], hp_t[ft], twbc[p])) == M[("B3h", li, ft)]
            for d in range(DT):
                bank = 5 + d % 2
                v.wait_ge(sem["sP"], M[("B3dn", li, d)])
                if li == 8 and d < 2:
                    v.wait_ge(sem["dStg0"], stg_half_mark[0])
                    v.wait_ge(sem["dStg1"], stg_half_mark[1])
                elif li > 0 or d >= 2:
                    pd = (li, d - 2) if d >= 2 else (li - 1, DT - 2 + d)
                    v.wait_ge(sem[f"dStg{d % 2}"], M[("B3st", pd[0], pd[1])])
                assert inc("sV", v.tensor_copy(stg[d % 2], pb[bank][:, :])) == M[("B3ev", li, d)]
        block.vector(_)

        def _(q, li=li, d_all=True, r=r, half=half):
            for d in range(DT):
                q.wait_ge(sem["sV"], M[("B3ev", li, d)])
                ins = q.dma_start(
                    out=part[half].ap()[r * D + d * HD:r * D + (d + 1) * HD, :],
                    in_=stg[d % 2])
                assert inc(f"dStg{d % 2}", ins) == M[("B3st", li, d)]
        block.sync(_)

        if li == 7:
            stg_half_mark = (C["dStg0"], C["dStg1"])

    def _(q):
        q.wait_ge(sem["dStg0"], stg_half_mark[0])
        q.wait_ge(sem["dStg1"], stg_half_mark[1])
        inc("cRsA", q.collective_compute(
            "ReduceScatter", ALU.add, replica_groups=[list(range(NCORE))],
            ins=[part[0].ap().opt()], outs=[rsout[0].ap().opt()]), 1)
        q.wait_ge(sem["dStg0"], C["dStg0"])
        q.wait_ge(sem["dStg1"], C["dStg1"])
        inc("cRsB", q.collective_compute(
            "ReduceScatter", ALU.add, replica_groups=[list(range(NCORE))],
            ins=[part[1].ap().opt()], outs=[rsout[1].ap().opt()]), 1)
    block.gpsimd(_)

    it = 0
    for j2 in range(2):
        for d in range(DT):
            p = it % 4

            def _(q, d=d, j2=j2, it=it, p=p):
                if it == 0:
                    q.wait_ge(sem["cRsA"], 1)
                if it == DT:
                    q.wait_ge(sem["cRsB"], 1)
                if it >= 4:
                    q.wait_ge(sem["sA"], M[("Cb", it - 4)])
                    q.wait_ge(sem[f"dCo{p}"], M[("Co", it - 4)])
                inc(f"dCl{p}", q.dma_start(
                    out=c_rs[p], in_=rsout[j2].ap()[d * HD:(d + 1) * HD, :]))
                inc(f"dCl{p}", q.dma_start(
                    out=c_x[p],
                    in_=xT[d * HD:(d + 1) * HD, j2 * BLK:(j2 + 1) * BLK]))
                M[("Cl", it)] = C[f"dCl{p}"]
            block.sync(_)

            def _(v, d=d, it=it, p=p):
                v.wait_ge(sem[f"dCl{p}"], M[("Cl", it)])
                M[("Ca", it)] = inc("sV", v.tensor_add(c_o[p], c_rs[p], c_x[p]))
            block.vector(_)

            def _(sc, d=d, j2=j2, it=it, p=p):
                sc.wait_ge(sem["sV"], M[("Ca", it)])
                M[("Cb", it)] = inc("sA", sc.activation(
                    c_o[p], c_o[p], AF.Identity,
                    bias=bo_sb[:, d:d + 1], scale=1.0))
                sc.wait_ge(sem["sA"], M[("Cb", it)])
                ins = sc.dma_start(
                    out=outT[d * HD:(d + 1) * HD, j2 * BLK:(j2 + 1) * BLK],
                    in_=c_o[p])
                M[("Co", it)] = inc(f"dCo{p}", ins)
            block.scalar(_)
            it += 1

    if debug:
        def _(q):
            for nm, (dst, src) in dbg.items():
                inc("dDbg", q.dma_start(out=dst, in_=src.ap()))
        block.sync(_)

    def _(q):
        for k in SEM_NAMES:
            if (k.startswith("d") or k.startswith("c")) and C[k]:
                q.wait_ge(sem[k], C[k])
    block.sync(_)

    block.__exit__(None, None, None)
    return nc


def prep_inputs(L0_attn_h, L0_attn_Wq, L0_attn_bq, L0_attn_Wk, L0_attn_bk,
                L0_attn_Wv, L0_attn_bv, L0_attn_Wo, L0_attn_bo,
                L0_ffn_gate_w, L0_ffn_gate_proj, L0_ffn_up_proj, L0_ffn_down_proj):
    x = np.ascontiguousarray(np.asarray(L0_attn_h, dtype=np.float32).reshape(NTOK, D).T)
    wqT = np.ascontiguousarray(np.asarray(L0_attn_Wq).T)
    wkT = np.ascontiguousarray(np.asarray(L0_attn_Wk).T)
    wvT = np.ascontiguousarray(np.asarray(L0_attn_Wv).T)
    woT = np.ascontiguousarray(np.asarray(L0_attn_Wo).T)
    gwT = np.ascontiguousarray(np.asarray(L0_ffn_gate_w).T)
    bq = np.asarray(L0_attn_bq); bk = np.asarray(L0_attn_bk)
    bv = np.asarray(L0_attn_bv); bo = np.asarray(L0_attn_bo)
    gp = np.asarray(L0_ffn_gate_proj); up = np.asarray(L0_ffn_up_proj)
    dp = np.asarray(L0_ffn_down_proj)

    mk = np.zeros((4, 128, BLK), dtype=BFNP)
    for m in range(4):
        i = np.arange(128)[:, None]
        jj = np.arange(BLK)[None, :]
        mk[m] = (128 * m + i <= jj).astype(BFNP)
    idf = np.eye(128, dtype=np.float32)

    in_maps = []
    for c in range(NCORE):
        hsl = slice(2 * c * HD, (2 * c + 2) * HD)
        sel = np.zeros((8, 128), np.float32)
        sel[c, :] = 1.0
        in_maps.append({
            "xT": np.ascontiguousarray(x[:, c * TPC:(c + 1) * TPC]),
            "wqT": np.ascontiguousarray(wqT[:, hsl]),
            "wkT": np.ascontiguousarray(wkT[:, hsl]),
            "wvT": np.ascontiguousarray(wvT[:, hsl]),
            "woT": np.ascontiguousarray(woT[hsl, :]),
            "bqh": np.ascontiguousarray(bq[hsl].reshape(NHC, HD).T),
            "bkh": np.ascontiguousarray(bk[hsl].reshape(NHC, HD).T),
            "bvr": np.ascontiguousarray(bv[hsl].reshape(1, NHC * HD)),
            "boh": np.ascontiguousarray(bo.reshape(DT, 128).T),
            "gwT": gwT,
            "gTd": np.ascontiguousarray(gp[c].T),
            "uTd": np.ascontiguousarray(up[c].T),
            "dTd": np.ascontiguousarray(dp[c].T),
            "selb": sel,
            "masks": mk,
            "idf32": idf,
        })
    return in_maps


_CACHE = {}


def kernel(**inputs):
    if "nc" not in _CACHE:
        _CACHE["nc"] = build_program(debug=False)
    nc = _CACHE["nc"]
    in_maps = prep_inputs(**inputs)
    res = run_bass_kernel_spmd(nc, in_maps, core_ids=list(range(NCORE)))
    shards = [r["outT"] for r in res.results]
    outT = np.concatenate(shards, axis=1)
    return np.ascontiguousarray(outT.T).reshape(2, S, D).astype(np.float32)


# version 04
# speedup vs baseline: 1.4220x; 1.2751x over previous
import sys

import numpy as np
import ml_dtypes

if "/opt/trn_rl_repo" not in sys.path:
    sys.path.insert(0, "/opt/trn_rl_repo")

import concourse.bass as bass
import concourse.mybir as mybir
from concourse.bass_utils import run_bass_kernel_spmd

F32 = mybir.dt.float32
BF16 = mybir.dt.bfloat16
AF = mybir.ActivationFunctionType
ALU = mybir.AluOpType
AX = mybir.AxisListType
BFNP = ml_dtypes.bfloat16

NCORE = 8
D = 2048
NTOK = 8192
TPC = NTOK // NCORE
S = 4096
HD = 128
NHC = 2
DT = D // 128
F = 1408
FT = F // 128
BLK = 512
NB = NTOK // BLK
QBN = S // BLK
SCALE = float(1.0 / np.sqrt(128.0))
KB = 1024

SEM_NAMES = ["sP", "sV", "sA",
             "dPre", "dX", "dAgx", "dAgt", "dMask",
             "dXn0", "dXn1", "dOtW0", "dOtW1", "dStg0", "dStg1",
             "dCl0", "dCl1", "dCl2", "dCl3", "dCo0", "dCo1", "dCo2", "dCo3", "dDbg",
             "dQkv", "dGate", "dUp", "dDown", "dWo",
             "cAgx", "cAgt", "cRsA", "cRsB"]


def build_program(debug=False, upto=None):
    nc = bass.Bass("TRN2", target_bir_lowering=False, debug=False)

    def din(name, shape, dt=F32):
        return nc.dram_tensor(name, shape, dt, kind="ExternalInput").ap()

    xT = din("xT", [D, TPC])
    wqT = din("wqT", [D, NHC * HD])
    wkT = din("wkT", [D, NHC * HD])
    wvT = din("wvT", [D, NHC * HD])
    woT = din("woT", [NHC * HD, D])
    bqh = din("bqh", [HD, NHC])
    bkh = din("bkh", [HD, NHC])
    bvr = din("bvr", [1, NHC * HD])
    boh = din("boh", [128, DT])
    gwT = din("gwT", [D, 8])
    gTd = din("gTd", [D, F])
    uTd = din("uTd", [D, F])
    dTd = din("dTd", [F, D])
    selb = din("selb", [8, 128])
    masks = din("masks", [4, 128, BLK], BF16)
    idf32 = din("idf32", [128, 128])

    outT = nc.dram_tensor("outT", [D, TPC], F32, kind="ExternalOutput").ap()

    agin_x = nc.dram_tensor("agin_x", [D, TPC], BF16)
    agout_x = nc.dram_tensor("agout_x", [NCORE * D, TPC], BF16)
    agin_t = nc.dram_tensor("agin_t", [8, TPC], F32)
    agout_t = nc.dram_tensor("agout_t", [NCORE * 8, TPC], F32)
    ot_d = nc.dram_tensor("ot_d", [NHC * HD, NTOK], BF16)
    part = [nc.dram_tensor(f"part{h}", [NCORE * D, BLK], F32) for h in range(2)]
    rsout = [nc.dram_tensor(f"rsout{h}", [D, BLK], F32) for h in range(2)]

    dbg = {}
    if debug:
        for nm, t in [("d_agx", agout_x), ("d_agt", agout_t), ("d_ot", ot_d),
                      ("d_part0", part[0]), ("d_part1", part[1]),
                      ("d_rs0", rsout[0]), ("d_rs1", rsout[1])]:
            dbg[nm] = (nc.dram_tensor(nm, list(t.ap().shape), t.ap().dtype,
                                      kind="ExternalOutput").ap(), t)

    ARENA = 205 * KB
    arena = nc.alloc_sbuf_tensor("arena", [128, ARENA], mybir.dt.uint8)

    def cv(off, nelem, dt, parts=128):
        nb = {F32: 4, BF16: 2}[dt]
        assert off % 64 == 0 and off + nelem * nb <= ARENA, f"bad carve {off}"
        return arena.ap()[0:parts, off:off + nelem * nb].bitcast(dt)

    o = 0
    onesm = cv(o, 128, F32); o += 512
    ones_col_bf = cv(o, 1, BF16); o += 64
    ones_row = cv(o, 128, F32, parts=1); o += 512
    neg8 = cv(o, 1, F32); o += 64
    eps128 = cv(o, 1, F32); o += 64
    bq_sb = cv(o, 2, F32); o += 64
    bk_sb = cv(o, 2, F32); o += 64
    bo_sb = cv(o, DT, F32); o += 64
    bv_bc = cv(o, NHC * HD, BF16); o += 512
    bvr_sb = cv(o, NHC * HD, F32, parts=1); o += 1024
    idf32_sb = cv(o, 128, F32); o += 512
    gwT_sb = cv(o, DT * 8, F32); o += 512
    selb_sb = cv(o, 128, F32, parts=8); o += 512
    assert o <= 6 * KB

    xnb = [cv(6 * KB + i * 16 * KB, DT * BLK, BF16) for i in range(2)]
    pt_t = [cv(6 * KB + i * KB, BLK, BF16) for i in range(4)]
    pt_pair = [cv(6 * KB + i * 2 * KB, 2 * BLK, BF16) for i in range(2)]
    mask_sb = [cv(10 * KB + i * KB, BLK, BF16) for i in range(4)]
    otv = [cv(14 * KB + i * KB, BLK, BF16) for i in range(2)]
    rec_bc = cv(16 * KB, BLK, BF16)
    rrow = [cv(17 * KB + 2048 * i, BLK, F32, parts=1) for i in range(2)]

    QT_sb = cv(38 * KB, NHC * NTOK, BF16)
    KT_sb = cv(70 * KB, NHC * NTOK, BF16)
    V_sb = cv(102 * KB, 64 * NHC * HD, BF16)
    xblk = cv(38 * KB, DT * BLK, F32)
    xn32 = cv(70 * KB, DT * BLK, F32)
    sqt = [cv(102 * KB + i * 2 * KB, BLK, F32) for i in range(2)]
    rstd_t = cv(106 * KB, BLK, F32)
    twt = [cv(108 * KB + i * 2 * KB, BLK, F32, parts=8) for i in range(2)]
    uT_sb = cv(38 * KB, DT * F, BF16)
    dT_sb = cv(82 * KB, FT * D, BF16)
    wo_sb = cv(126 * KB, NHC * D, BF16)
    c_rs = [cv(38 * KB + i * 2 * KB, BLK, F32) for i in range(4)]
    c_x = [cv(46 * KB + i * 2 * KB, BLK, F32) for i in range(4)]
    c_o = [cv(54 * KB + i * 2 * KB, BLK, F32) for i in range(4)]

    gT_sb = cv(134 * KB, DT * F, BF16)

    wq_sb = cv(178 * KB, DT * NHC * HD, BF16)
    wk_sb = cv(186 * KB, DT * NHC * HD, BF16)
    wv_sb = cv(194 * KB, DT * NHC * HD, BF16)
    hp_t = [cv(178 * KB + i * KB, BLK, BF16) for i in range(FT)]
    otin = [cv(189 * KB + i * KB, BLK, BF16) for i in range(4)]
    stg = [cv(193 * KB + i * 2 * KB, BLK, F32) for i in range(2)]
    sg_t = [cv(197 * KB + i * KB, BLK, BF16) for i in range(2)]
    twbc = [cv(199 * KB + i * KB, BLK, BF16) for i in range(2)]
    twtile = [cv(201 * KB + i * 2 * KB, BLK, F32, parts=8) for i in range(2)]

    psS = [nc.alloc_psum_tensor(f"psS{i}", [128, 2 * BLK], F32) for i in range(2)]
    pbO = [nc.alloc_psum_tensor(f"pbO{i}", [128, BLK], F32) for i in range(2)]
    pbBc = nc.alloc_psum_tensor("pbBc", [128, BLK], F32)
    pbR = nc.alloc_psum_tensor("pbR", [128, BLK], F32)
    pb = [psS[0].ap()[:, 0:BLK], psS[0].ap()[:, BLK:2 * BLK],
          psS[1].ap()[:, 0:BLK], psS[1].ap()[:, BLK:2 * BLK],
          pbBc.ap(), pbO[0].ap(), pbO[1].ap(), pbR.ap()]

    C = {k: 0 for k in SEM_NAMES}
    sem = {k: nc.semaphore(k).__enter__() for k in SEM_NAMES}

    def inc(name, ins, n=None):
        if n is None:
            n = 1 if name in ("sP", "sV", "sA") or name.startswith("c") else 16
        ins.then_inc(sem[name], n)
        C[name] += n
        return C[name]

    M = {}
    block = nc.Block().__enter__()

    wqv = wq_sb.rearrange("p (t h) -> p t h", t=DT)
    wkv = wk_sb.rearrange("p (t h) -> p t h", t=DT)
    wvv = wv_sb.rearrange("p (t h) -> p t h", t=DT)
    gwv = gwT_sb.rearrange("p (t e) -> p t e", t=DT)
    gv = gT_sb.rearrange("p (t f) -> p t f", t=DT)
    uv = uT_sb.rearrange("p (t f) -> p t f", t=DT)
    dvw = dT_sb.rearrange("p (t d) -> p t d", t=FT)
    wov = wo_sb.rearrange("p (h d) -> p h d", h=NHC)
    xnv = [bb.rearrange("p (t q) -> p t q", t=DT) for bb in xnb]

    def _(q):
        inc("dPre", q.dma_start(out=idf32_sb, in_=idf32))
        inc("dPre", q.dma_start(out=bq_sb, in_=bqh))
        inc("dPre", q.dma_start(out=bk_sb, in_=bkh))
        inc("dPre", q.dma_start(out=bo_sb, in_=boh))
        inc("dPre", q.dma_start(out=bvr_sb, in_=bvr))
        inc("dPre", q.dma_start(out=gwv, in_=gwT.rearrange("(t p) e -> p t e", p=128)))
        inc("dPre", q.dma_start(out=selb_sb, in_=selb))
    block.sync(_)
    PRE_ALL = C["dPre"]

    def _(q):
        inc("dQkv", q.dma_start(out=wqv, in_=wqT.rearrange("(t p) h -> p t h", p=128)))
        inc("dQkv", q.dma_start(out=wkv, in_=wkT.rearrange("(t p) h -> p t h", p=128)))
        inc("dQkv", q.dma_start(out=wvv, in_=wvT.rearrange("(t p) h -> p t h", p=128)))
        inc("dGate", q.dma_start(out=gv, in_=gTd.rearrange("(t p) f -> p t f", p=128)))
    block.gpsimd(_)

    def _(v):
        v.memset(onesm, 1.0)
        v.memset(ones_col_bf, 1.0)
        v.memset(ones_row, 1.0)
        v.memset(neg8, -8.0)
        M["consts"] = inc("sV", v.memset(eps128, 1e-5))
    block.vector(_)

    def _(t):
        t.wait_ge(sem["dPre"], PRE_ALL)
        t.wait_ge(sem["sV"], M["consts"])
        M["bv_pe"] = inc("sP", t.matmul(pb[7][:, 0:NHC * HD], ones_row, bvr_sb,
                                        start=True, stop=True))
    block.tensor(_)

    def _(v):
        v.wait_ge(sem["sP"], M["bv_pe"])
        M["bv"] = inc("sV", v.tensor_copy(bv_bc, pb[7][:, 0:NHC * HD]))
    block.vector(_)

    for j in range(2):
        def _(q, j=j):
            if j > 0:
                q.wait_ge(sem["sP"], M[("Arpe", j - 1, 3)])
            inc("dX", q.dma_start(
                out=xblk.rearrange("p (t q) -> p t q", t=DT),
                in_=xT.rearrange("(t p) q -> p t q", p=128)[:, :, j * BLK:(j + 1) * BLK]))
        block.sync(_)

        def _(t, j=j):
            t.wait_ge(sem["dX"], (j + 1) * 16)
            t.wait_ge(sem["sV"], M["consts"])
            if j > 0:
                t.wait_ge(sem["sV"], M[("Axn1", j - 1)])
            for i in range(DT):
                ins = t.matmul(pb[0][:, :], onesm,
                               xblk[:, i * BLK:(i + 1) * BLK],
                               start=(i == 0), stop=(i == DT - 1))
            M[("Asum", j)] = inc("sP", ins)
        block.tensor(_)

        for i in range(DT):
            def _(sc, j=j, i=i):
                if i == 0:
                    sc.wait_ge(sem["dX"], (j + 1) * 16)
                    if j > 0:
                        sc.wait_ge(sem["sV"], M[("Atw", j - 1, 3)])
                        sc.wait_ge(sem["sP"], M[("AtwT", j - 1, 3)])
                if i >= 2:
                    sc.wait_ge(sem["sP"], M[("Asqpe", j, i - 2)])
                M[("Asq", j, i)] = inc("sA", sc.activation(
                    sqt[i % 2], xblk[:, i * BLK:(i + 1) * BLK], AF.Square))
            block.scalar(_)

            def _(t, j=j, i=i):
                t.wait_ge(sem["sA"], M[("Asq", j, i)])
                M[("Asqpe", j, i)] = inc("sP", t.matmul(
                    pb[1][:, :], onesm, sqt[i % 2],
                    start=(i == 0), stop=(i == DT - 1)))
            block.tensor(_)

        def _(v, j=j):
            v.wait_ge(sem["sP"], M[("Asum", j)])
            v.wait_ge(sem["sP"], M[("Asqpe", j, DT - 1)])
            w = inc("sV", v.tensor_scalar_mul(sqt[0], pb[0][:, :], 1.0 / D))
            w = inc("sV", v.tensor_scalar_mul(sqt[1], pb[1][:, :], 1.0 / D))
            v.wait_ge(sem["sV"], w)
            w = inc("sV", v.tensor_mul(sqt[0], sqt[0], sqt[0]))
            v.wait_ge(sem["sV"], w)
            M[("Avar", j)] = inc("sV", v.tensor_sub(sqt[1], sqt[1], sqt[0]))
        block.vector(_)

        def _(sc, j=j):
            sc.wait_ge(sem["sV"], M[("Avar", j)])
            M[("Astd", j)] = inc("sA", sc.activation(rstd_t, sqt[1], AF.Sqrt,
                                                     bias=eps128, scale=1.0))
        block.scalar(_)

        def _(v, j=j):
            v.wait_ge(sem["sA"], M[("Astd", j)])
            M[("Arstd", j)] = inc("sV", v.reciprocal(rstd_t, rstd_t))
            v.wait_ge(sem["sV"], M[("Arstd", j)])
            for i in range(DT):
                ins = v.scalar_tensor_tensor(xn32[:, i * BLK:(i + 1) * BLK],
                                             pb[0][:, :], -1.0 / D,
                                             xblk[:, i * BLK:(i + 1) * BLK],
                                             ALU.mult, ALU.add)
            w = inc("sV", ins)
            v.wait_ge(sem["sV"], w)
            for i in range(DT):
                ins = v.tensor_mul(xn32[:, i * BLK:(i + 1) * BLK],
                                   xn32[:, i * BLK:(i + 1) * BLK], rstd_t)
            M[("Axn1", j)] = inc("sV", ins)
            v.wait_ge(sem["sV"], M[("Axn1", j)])
            for i in range(DT):
                ins = v.tensor_copy(xnb[j][:, i * BLK:(i + 1) * BLK],
                                    xn32[:, i * BLK:(i + 1) * BLK])
            M[("Axncp", j)] = inc("sV", ins)
        block.vector(_)

        def _(q, j=j):
            q.wait_ge(sem["sV"], M[("Axncp", j)])
            inc("dAgx", q.dma_start(
                out=agin_x.ap().rearrange("(t p) q -> p t q", p=128)[:, :, j * BLK:(j + 1) * BLK],
                in_=xnv[j]))
        block.sync(_)

        for tt in range(4):
            def _(t, j=j, tt=tt):
                if tt == 0:
                    t.wait_ge(sem["dPre"], PRE_ALL)
                    t.wait_ge(sem["sV"], M[("Axn1", j)])
                prev = (j, tt - 1) if tt > 0 else ((j - 1, 3) if j > 0 else None)
                if prev is not None:
                    t.wait_ge(sem["sV"], M[("ArL", prev[0], prev[1])])
                for i in range(DT):
                    ins = t.matmul(pb[4][0:128, 0:8],
                                   xn32[:, i * BLK + tt * 128:i * BLK + tt * 128 + 128],
                                   gwv[:, i, :], start=(i == 0), stop=(i == DT - 1))
                M[("Arpe", j, tt)] = inc("sP", ins)
            block.tensor(_)

            def _(v, j=j, tt=tt):
                L = sqt[0][:, 0:8]; m1 = sqt[0][:, 8:9]; k1 = sqt[0][:, 16:24]
                L2 = sqt[0][:, 24:32]; m2 = sqt[0][:, 32:33]; k2 = sqt[0][:, 40:48]
                dd = sqt[0][:, 48:49]
                v.wait_ge(sem["sP"], M[("Arpe", j, tt)])
                M[("ArL", j, tt)] = w = inc("sV", v.tensor_copy(L, pb[4][0:128, 0:8]))
                v.wait_ge(sem["sV"], w)
                w = inc("sV", v.reduce_max(m1, L, axis=AX.X))
                v.wait_ge(sem["sV"], w)
                w = inc("sV", v.tensor_scalar(k1, L, m1, None, ALU.is_equal))
                v.wait_ge(sem["sV"], w)
                w = inc("sV", v.scalar_tensor_tensor(L2, k1, -1e30, L, ALU.mult, ALU.add))
                v.wait_ge(sem["sV"], w)
                w = inc("sV", v.reduce_max(m2, L2, axis=AX.X))
                v.wait_ge(sem["sV"], w)
                v.tensor_scalar(k2, L2, m2, None, ALU.is_equal)
                M[("Ar1", j, tt)] = inc("sV", v.tensor_sub(dd, m2, m1))
            block.vector(_)

            def _(sc, j=j, tt=tt):
                sc.wait_ge(sem["sV"], M[("Ar1", j, tt)])
                M[("Ar2", j, tt)] = inc("sA", sc.activation(
                    sqt[0][:, 49:50], sqt[0][:, 48:49], AF.Exp))
            block.scalar(_)

            def _(v, j=j, tt=tt):
                k1 = sqt[0][:, 16:24]; k2 = sqt[0][:, 40:48]
                ee = sqt[0][:, 49:50]; rr = sqt[0][:, 50:51]; w2 = sqt[0][:, 51:52]
                tw = sqt[1][:, 0:8]
                v.wait_ge(sem["sA"], M[("Ar2", j, tt)])
                w = inc("sV", v.tensor_scalar_add(rr, ee, 1.0))
                v.wait_ge(sem["sV"], w)
                w = inc("sV", v.reciprocal(rr, rr))
                v.wait_ge(sem["sV"], w)
                w = inc("sV", v.tensor_mul(w2, ee, rr))
                v.wait_ge(sem["sV"], w)
                prev = (j, tt - 1) if tt > 0 else ((j - 1, 3) if j > 0 else None)
                if prev is not None:
                    v.wait_ge(sem["sP"], M[("AtwT", prev[0], prev[1])])
                w = inc("sV", v.tensor_scalar(tw, k1, rr, None, ALU.mult))
                v.wait_ge(sem["sV"], w)
                M[("Atw", j, tt)] = inc("sV", v.scalar_tensor_tensor(
                    tw, k2, w2, tw, ALU.mult, ALU.add))
            block.vector(_)

            def _(t, j=j, tt=tt):
                t.wait_ge(sem["sV"], M[("Atw", j, tt)])
                M[("AtwT", j, tt)] = inc("sP", t.transpose(
                    pb[5][0:8, 0:128], sqt[1][:, 0:8], idf32_sb))
            block.tensor(_)

            def _(v, j=j, tt=tt):
                v.wait_ge(sem["sP"], M[("AtwT", j, tt)])
                M[("Atwc", j, tt)] = inc("sV", v.tensor_copy(
                    twt[j][0:8, tt * 128:(tt + 1) * 128], pb[5][0:8, 0:128]))
            block.vector(_)

        def _(q, j=j):
            q.wait_ge(sem["sV"], M[("Atwc", j, 3)])
            inc("dAgt", q.dma_start(out=agin_t.ap()[0:8, j * BLK:(j + 1) * BLK],
                                    in_=twt[j][0:8, :]))
        block.sync(_)

    if upto == "A":
        dbgA = {}
        for nm, shape, dt in [("t_xblk", [128, DT * BLK], F32),
                              ("t_xn32", [128, DT * BLK], F32),
                              ("t_srow", [128, BLK], F32),
                              ("t_agx", [D, TPC], BF16),
                              ("t_agt", [8, TPC], F32)]:
            dbgA[nm] = nc.dram_tensor(nm, shape, dt, kind="ExternalOutput").ap()

        def _(q):
            q.wait_ge(sem["sV"], M[("Axncp", 1)])
            q.wait_ge(sem["sV"], M[("Atwc", 1, 3)])
            q.wait_ge(sem["dAgx"], C["dAgx"])
            q.wait_ge(sem["dAgt"], C["dAgt"])
            inc("dDbg", q.dma_start(out=dbgA["t_xblk"], in_=xblk))
            inc("dDbg", q.dma_start(out=dbgA["t_xn32"], in_=xn32))
            inc("dDbg", q.dma_start(out=dbgA["t_srow"], in_=rstd_t))
            inc("dDbg", q.dma_start(out=dbgA["t_agx"], in_=agin_x.ap()))
            inc("dDbg", q.dma_start(out=dbgA["t_agt"], in_=agin_t.ap()))
            q.wait_ge(sem["dDbg"], C["dDbg"])
            q.wait_ge(sem["dQkv"], C["dQkv"])
            q.wait_ge(sem["dGate"], C["dGate"])
        block.sync(_)
        block.__exit__(None, None, None)
        return nc

    def _(q):
        q.wait_ge(sem["dAgx"], 32)
        inc("cAgx", q.collective_compute(
            "AllGather", ALU.bypass, replica_groups=[list(range(NCORE))],
            ins=[agin_x.ap().opt()], outs=[agout_x.ap().opt()]), 1)
        q.wait_ge(sem["dAgt"], 32)
        inc("cAgt", q.collective_compute(
            "AllGather", ALU.bypass, replica_groups=[list(range(NCORE))],
            ins=[agin_t.ap().opt()], outs=[agout_t.ap().opt()]), 1)
    block.gpsimd(_)

    def xn_load(q, tb):
        p = tb % 2
        r, half = tb // 2, tb % 2
        src = agout_x.ap()[r * D:(r + 1) * D, half * BLK:(half + 1) * BLK]
        inc(f"dXn{p}", q.dma_start(out=xnv[p], in_=src.rearrange("(t p) q -> p t q", p=128)))

    for tb in range(NB):
        p = tb % 2

        def _(q, tb=tb, p=p):
            if tb == 0:
                q.wait_ge(sem["cAgx"], 1)
            if tb >= 2:
                q.wait_ge(sem["sP"], M[("B1pe", tb - 2)])
            xn_load(q, tb)
            M[("B1x", tb)] = C[f"dXn{p}"]
        block.sync(_)

        def _(t, tb=tb, p=p):
            t.wait_ge(sem[f"dXn{p}"], M[("B1x", tb)])
            if tb == 0:
                t.wait_ge(sem["dQkv"], 48)
            if tb >= 1:
                t.wait_ge(sem["sA"], M[("B1evA", tb - 1)])
                t.wait_ge(sem["sV"], M[("B1evV", tb - 1)])
            xv = xnv[p]
            for h in range(NHC):
                for i in range(DT):
                    ins = t.matmul(pb[0 + h][:, :], wqv[:, i, h * HD:(h + 1) * HD],
                                   xv[:, i, :], start=(i == 0), stop=(i == DT - 1))
                M[("B1q", tb, h)] = inc("sP", ins)
                for i in range(DT):
                    ins = t.matmul(pb[2 + h][:, :], wkv[:, i, h * HD:(h + 1) * HD],
                                   xv[:, i, :], start=(i == 0), stop=(i == DT - 1))
                M[("B1k", tb, h)] = inc("sP", ins)
            for s4 in range(4):
                bank, half = 4 + s4 // 2, s4 % 2
                for i in range(DT):
                    ins = t.matmul(pb[bank][:, half * 256:(half + 1) * 256],
                                   xv[:, i, s4 * 128:(s4 + 1) * 128],
                                   wvv[:, i, :], start=(i == 0), stop=(i == DT - 1))
                if s4 % 2 == 1:
                    M[("B1v", tb, s4 // 2)] = inc("sP", ins)
            M[("B1pe", tb)] = C["sP"]
        block.tensor(_)

        def _(sc, tb=tb):
            if tb == 0:
                sc.wait_ge(sem["dPre"], PRE_ALL)
            for h in range(NHC):
                sc.wait_ge(sem["sP"], M[("B1q", tb, h)])
                sc.activation(QT_sb[:, h * NTOK + tb * BLK:h * NTOK + (tb + 1) * BLK],
                              pb[0 + h][:, :], AF.Identity, bias=bq_sb[:, h:h + 1],
                              scale=1.0)
                sc.wait_ge(sem["sP"], M[("B1k", tb, h)])
                ins = sc.activation(KT_sb[:, h * NTOK + tb * BLK:h * NTOK + (tb + 1) * BLK],
                                    pb[2 + h][:, :], AF.Identity, bias=bk_sb[:, h:h + 1],
                                    scale=1.0)
            M[("B1evA", tb)] = inc("sA", ins)
        block.scalar(_)

        def _(v, tb=tb):
            if tb == 0:
                v.wait_ge(sem["sV"], M["bv"])
                v.wait_ge(sem["dAgt"], 32)
            for s4 in range(4):
                bank, half = 4 + s4 // 2, s4 % 2
                g = tb * 4 + s4
                v.wait_ge(sem["sP"], M[("B1v", tb, s4 // 2)])
                ins = v.tensor_add(V_sb[:, g * 256:(g + 1) * 256],
                                   pb[bank][:, half * 256:(half + 1) * 256], bv_bc)
            M[("B1evV", tb)] = inc("sV", ins)
        block.vector(_)

    def _(q):
        q.wait_ge(sem["sP"], M[("B1pe", 14)])
        for m in range(4):
            inc("dMask", q.dma_start(out=mask_sb[m], in_=masks[m]))
    block.sync(_)

    baseP, baseA, baseV = C["sP"], C["sA"], C["sV"]
    sMp, pvMp, expMp = {}, {}, {}
    bcM, maskM = {}, {}
    rrowM, recM, recbcM, otvM = {}, {}, {}, {}
    otdM = {}
    p_, a_, v_ = baseP, baseA, baseV
    ot_use = {0: C["dOtW0"], 1: C["dOtW1"]}
    gq = 0
    gp = 0
    for u in range(4):
        for qb in range(QBN):
            nk = 4 * qb + 4
            np_ = nk // 2
            order = [("s", 0)] + ([("s", 1)] if np_ > 1 else [])
            for pk in range(np_):
                if pk + 2 < np_:
                    order.append(("s", pk + 2))
                order.append(("pv", pk))
            for kind, pk in order:
                p_ += 1
                if kind == "s":
                    sMp[gp + pk] = p_
                else:
                    pvMp[gp + pk] = p_
            for pk in range(np_):
                expMp[gp + pk] = a_ + 1
                a_ += 1
            for kt in range(4 * qb, nk):
                maskM[(gq, kt)] = v_ + 1
                v_ += 1
            bcM[gq] = p_ + 1
            p_ += 1
            rrowM[gq] = v_ + 1
            recM[gq] = v_ + 2
            recbcM[gq] = v_ + 3
            otvM[gq] = v_ + 4
            v_ += 4
            ot_use[gq % 2] += 16
            otdM[gq] = ot_use[gq % 2]
            gp += np_
            gq += 1

    gq = 0
    gp = 0
    for u in range(4):
        b, hh = u // 2, u % 2
        qoff = hh * NTOK + b * S
        for qb in range(QBN):
            nk = 4 * qb + 4
            np_ = nk // 2
            g0, gq0 = gp, gq

            def _(sc, u=u, qb=qb, nk=nk, np_=np_, g0=g0):
                for pk in range(np_):
                    gg = g0 + pk
                    sc.wait_ge(sem["sP"], sMp[gg])
                    if gg >= 2:
                        sc.wait_ge(sem["sP"], pvMp[gg - 2])
                    ins = sc.activation(pt_pair[gg % 2], psS[gg % 2].ap()[:, :],
                                        AF.Exp, bias=neg8, scale=SCALE)
                    assert inc("sA", ins) == expMp[gg]
            block.scalar(_)

            def _(v, u=u, qb=qb, nk=nk, np_=np_, g0=g0, gq0=gq0):
                if u == 0 and qb == 0:
                    v.wait_ge(sem["dMask"], 64)
                for kt in range(4 * qb, nk):
                    pk = kt // 2
                    gg = g0 + pk
                    v.wait_ge(sem["sA"], expMp[gg])
                    half = pt_pair[gg % 2][:, (kt % 2) * BLK:(kt % 2 + 1) * BLK]
                    ins = v.tensor_mul(half, half, mask_sb[kt - 4 * qb])
                    assert inc("sV", ins) == maskM[(gq0, kt)]
                v.wait_ge(sem["sP"], pvMp[g0 + np_ - 1])
                if gq0 >= 2:
                    v.wait_ge(sem["sP"], bcM[gq0 - 2])
                assert inc("sV", v.tensor_copy(rrow[gq0 % 2], pb[7][0:1, :])) == rrowM[gq0]
                v.wait_ge(sem["sV"], rrowM[gq0])
                assert inc("sV", v.reciprocal(rrow[gq0 % 2], rrow[gq0 % 2])) == recM[gq0]
                v.wait_ge(sem["sP"], bcM[gq0])
                assert inc("sV", v.tensor_copy(rec_bc, pb[4][:, :])) == recbcM[gq0]
                v.wait_ge(sem["sV"], recbcM[gq0])
                if gq0 >= 2:
                    v.wait_ge(sem[f"dOtW{gq0 % 2}"], otdM[gq0 - 2])
                ins = v.tensor_mul(otv[gq0 % 2], pbO[gq0 % 2].ap()[:, :], rec_bc)
                assert inc("sV", ins) == otvM[gq0]
            block.vector(_)

            def _(t, u=u, qb=qb, nk=nk, np_=np_, g0=g0, gq0=gq0, hh=hh, b=b, qoff=qoff):
                if u == 0 and qb == 0:
                    t.wait_ge(sem["sA"], M[("B1evA", NB - 1)])
                    t.wait_ge(sem["sV"], M[("B1evV", NB - 1)])
                if gq0 >= 2:
                    t.wait_ge(sem["sV"], otvM[gq0 - 2])
                if gq0 >= 1:
                    t.wait_ge(sem["sV"], rrowM[gq0 - 1])

                def s_pair(pk):
                    gg = g0 + pk
                    if gg >= 2:
                        t.wait_ge(sem["sA"], expMp[gg - 2])
                    for half in range(2):
                        kt = 2 * pk + half
                        ins = t.matmul(psS[gg % 2].ap()[:, half * BLK:(half + 1) * BLK],
                                       KT_sb[:, qoff + kt * HD:qoff + (kt + 1) * HD],
                                       QT_sb[:, qoff + qb * BLK:qoff + (qb + 1) * BLK],
                                       start=True, stop=True)
                    assert inc("sP", ins) == sMp[gg]

                def pv_block(pk):
                    gg = g0 + pk
                    t.wait_ge(sem["sA"], expMp[gg])
                    for half in range(2):
                        kt = 2 * pk + half
                        if kt >= 4 * qb:
                            t.wait_ge(sem["sV"], maskM[(gq0, kt)])
                    for half in range(2):
                        kt = 2 * pk + half
                        pth = pt_pair[gg % 2][:, half * BLK:(half + 1) * BLK]
                        t.matmul(pbO[gq0 % 2].ap()[:, :],
                                 V_sb[:, (b * 32 + kt) * 256 + hh * HD:(b * 32 + kt) * 256 + (hh + 1) * HD],
                                 pth, start=(kt == 0), stop=(kt == nk - 1))
                        ins = t.matmul(pb[7][0:1, :], ones_col_bf, pth,
                                       start=(kt == 0), stop=(kt == nk - 1))
                    assert inc("sP", ins) == pvMp[gg]

                s_pair(0)
                if np_ > 1:
                    s_pair(1)
                for pk in range(np_):
                    if pk + 2 < np_:
                        s_pair(pk + 2)
                    pv_block(pk)
                t.wait_ge(sem["sV"], recM[gq0])
                if gq0 >= 1:
                    t.wait_ge(sem["sV"], recbcM[gq0 - 1])
                ins = t.matmul(pb[4][:, :], ones_row, rrow[gq0 % 2], start=True, stop=True)
                assert inc("sP", ins) == bcM[gq0]
            block.tensor(_)

            def _(q, gq0=gq0, hh=hh, b=b, qb=qb):
                q.wait_ge(sem["sV"], otvM[gq0])
                ins = q.dma_start(
                    out=ot_d.ap()[hh * HD:(hh + 1) * HD,
                                  b * S + qb * BLK:b * S + (qb + 1) * BLK],
                    in_=otv[gq0 % 2])
                assert inc(f"dOtW{gq0 % 2}", ins) == otdM[gq0]
            block.sync(_)

            gp += np_
            gq += 1

    M["B2peEnd"] = C["sP"]
    M["B2vEnd"] = C["sV"]
    OTW_ALL = (C["dOtW0"], C["dOtW1"])

    def _(q):
        q.wait_ge(sem["sP"], M["B2peEnd"])
        inc("dUp", q.dma_start(out=uv, in_=uTd.rearrange("(t p) f -> p t f", p=128)))
        inc("dDown", q.dma_start(out=dvw, in_=dTd.rearrange("(t p) d -> p t d", p=128)))
        inc("dWo", q.dma_start(out=wov, in_=woT.rearrange("(h p) d -> p h d", p=128)))
    block.gpsimd(_)

    tborder = [2 * i for i in range(8)] + [2 * i + 1 for i in range(8)]
    stg_half_mark = None
    baseP3, baseA3, baseV3 = C["sP"], C["sA"], C["sV"]
    stg_base = {0: C["dStg0"], 1: C["dStg1"]}
    for li in range(NB):
        M[("B3twpe", li)] = baseP3 + li * (1 + FT + DT) + 1
        for ft in range(FT):
            M[("B3gu", li, ft)] = baseP3 + li * (1 + FT + DT) + 1 + ft + 1
            M[("B3sg", li, ft)] = baseA3 + li * FT + ft + 1
        for d in range(DT):
            M[("B3dn", li, d)] = baseP3 + li * (1 + FT + DT) + 1 + FT + d + 1
        M[("B3twc", li)] = baseV3 + li * (1 + 2 * FT + DT) + 1
        for ft in range(FT):
            M[("B3h", li, ft)] = baseV3 + li * (1 + 2 * FT + DT) + 1 + 2 * (ft + 1)
        for d in range(DT):
            M[("B3ev", li, d)] = baseV3 + li * (1 + 2 * FT + DT) + 1 + 2 * FT + d + 1
        for d in range(DT):
            sp = d % 2
            stg_base[sp] += 16
            M[("B3st", li, d)] = stg_base[sp]
    for li, tb in enumerate(tborder):
        p = li % 2
        r, half = tb // 2, tb % 2

        def _(q, li=li, tb=tb, p=p, r=r, half=half):
            if li == 0:
                q.wait_ge(sem["sP"], M["B2peEnd"])
                q.wait_ge(sem["sV"], M["B2vEnd"])
                q.wait_ge(sem["dOtW0"], OTW_ALL[0])
                q.wait_ge(sem["dOtW1"], OTW_ALL[1])
                q.wait_ge(sem["cAgt"], 1)
            if li >= 2:
                q.wait_ge(sem["sP"], M[("B3gu", li - 2, FT - 1)])
                q.wait_ge(sem["sP"], M[("B3dn", li - 2, DT - 1)])
            src = agout_x.ap()[r * D:(r + 1) * D, half * BLK:(half + 1) * BLK]
            inc(f"dXn{p}", q.dma_start(out=xnv[p],
                                       in_=src.rearrange("(t p) q -> p t q", p=128)))
            for hh in range(NHC):
                inc(f"dXn{p}", q.dma_start(
                    out=otin[2 * p + hh],
                    in_=ot_d.ap()[hh * HD:(hh + 1) * HD, tb * BLK:(tb + 1) * BLK]))
            inc(f"dXn{p}", q.dma_start(
                out=twtile[p],
                in_=agout_t.ap()[r * 8:(r + 1) * 8, half * BLK:(half + 1) * BLK]))
            M[("B3ld", li)] = C[f"dXn{p}"]
        block.sync(_)

        def _(t, li=li, tb=tb, p=p):
            t.wait_ge(sem[f"dXn{p}"], M[("B3ld", li)])
            if li == 0:
                t.wait_ge(sem["dGate"], 16)
                t.wait_ge(sem["sV"], M["B2vEnd"])
            if li >= 1:
                t.wait_ge(sem["sV"], M[("B3twc", li - 1)])
            assert inc("sP", t.matmul(
                pb[4][:, :], selb_sb, twtile[p],
                start=True, stop=True)) == M[("B3twpe", li)]
            xv = xnv[p]
            for ft in range(FT):
                gb, ub = 0 + ft % 2, 2 + ft % 2
                if li > 0 or ft >= 2:
                    pft = (li, ft - 2) if ft >= 2 else (li - 1, FT - 2 + ft)
                    t.wait_ge(sem["sA"], M[("B3sg", pft[0], pft[1])])
                    t.wait_ge(sem["sV"], M[("B3h", pft[0], pft[1])])
                for i in range(DT):
                    t.matmul(pb[gb][:, :], gv[:, i, ft * HD:(ft + 1) * HD],
                             xv[:, i, :], start=(i == 0), stop=(i == DT - 1))
                if ft == 0 and li == 0:
                    t.wait_ge(sem["dUp"], 16)
                for i in range(DT):
                    ins = t.matmul(pb[ub][:, :], uv[:, i, ft * HD:(ft + 1) * HD],
                                   xv[:, i, :], start=(i == 0), stop=(i == DT - 1))
                assert inc("sP", ins) == M[("B3gu", li, ft)]
            for d in range(DT):
                bank = 5 + d % 2
                if d == 0:
                    t.wait_ge(sem["sV"], M[("B3h", li, FT - 1)])
                    if li == 0:
                        t.wait_ge(sem["dDown"], 16)
                        t.wait_ge(sem["dWo"], 16)
                if li > 0 or d >= 2:
                    pd = (li, d - 2) if d >= 2 else (li - 1, DT - 2 + d)
                    t.wait_ge(sem["sV"], M[("B3ev", pd[0], pd[1])])
                for ft in range(FT):
                    t.matmul(pb[bank][:, :], dvw[:, ft, d * HD:(d + 1) * HD],
                             hp_t[ft], start=(ft == 0), stop=False)
                for hh in range(NHC):
                    ins = t.matmul(pb[bank][:, :], wov[:, hh, d * HD:(d + 1) * HD],
                                   otin[2 * p + hh],
                                   start=False, stop=(hh == NHC - 1))
                assert inc("sP", ins) == M[("B3dn", li, d)]
        block.tensor(_)

        def _(sc, li=li):
            for ft in range(FT):
                sc.wait_ge(sem["sP"], M[("B3gu", li, ft)])
                assert inc("sA", sc.activation(
                    sg_t[ft % 2], pb[0 + ft % 2][:, :], AF.Silu)) == M[("B3sg", li, ft)]
        block.scalar(_)

        def _(v, li=li, p=p, half=half):
            v.wait_ge(sem["sP"], M[("B3twpe", li)])
            assert inc("sV", v.tensor_copy(twbc[p], pb[4][:, :])) == M[("B3twc", li)]
            for ft in range(FT):
                v.wait_ge(sem["sA"], M[("B3sg", li, ft)])
                if li >= 1 and ft == 0:
                    v.wait_ge(sem["sP"], M[("B3dn", li - 1, DT - 1)])
                w = inc("sV", v.tensor_mul(hp_t[ft], pb[2 + ft % 2][:, :], sg_t[ft % 2]))
                v.wait_ge(sem["sV"], w)
                assert inc("sV", v.tensor_mul(
                    hp_t[ft], hp_t[ft], twbc[p])) == M[("B3h", li, ft)]
            for d in range(DT):
                bank = 5 + d % 2
                v.wait_ge(sem["sP"], M[("B3dn", li, d)])
                if li == 8 and d < 2:
                    v.wait_ge(sem["dStg0"], stg_half_mark[0])
                    v.wait_ge(sem["dStg1"], stg_half_mark[1])
                elif li > 0 or d >= 2:
                    pd = (li, d - 2) if d >= 2 else (li - 1, DT - 2 + d)
                    v.wait_ge(sem[f"dStg{d % 2}"], M[("B3st", pd[0], pd[1])])
                assert inc("sV", v.tensor_copy(stg[d % 2], pb[bank][:, :])) == M[("B3ev", li, d)]
        block.vector(_)

        def _(q, li=li, d_all=True, r=r, half=half):
            for d in range(DT):
                q.wait_ge(sem["sV"], M[("B3ev", li, d)])
                ins = q.dma_start(
                    out=part[half].ap()[r * D + d * HD:r * D + (d + 1) * HD, :],
                    in_=stg[d % 2])
                assert inc(f"dStg{d % 2}", ins) == M[("B3st", li, d)]
        block.sync(_)

        if li == 7:
            stg_half_mark = (C["dStg0"], C["dStg1"])

    def _(q):
        q.wait_ge(sem["dStg0"], stg_half_mark[0])
        q.wait_ge(sem["dStg1"], stg_half_mark[1])
        inc("cRsA", q.collective_compute(
            "ReduceScatter", ALU.add, replica_groups=[list(range(NCORE))],
            ins=[part[0].ap().opt()], outs=[rsout[0].ap().opt()]), 1)
        q.wait_ge(sem["dStg0"], C["dStg0"])
        q.wait_ge(sem["dStg1"], C["dStg1"])
        inc("cRsB", q.collective_compute(
            "ReduceScatter", ALU.add, replica_groups=[list(range(NCORE))],
            ins=[part[1].ap().opt()], outs=[rsout[1].ap().opt()]), 1)
    block.gpsimd(_)

    it = 0
    for j2 in range(2):
        for d in range(DT):
            p = it % 4

            def _(q, d=d, j2=j2, it=it, p=p):
                if it == 0:
                    q.wait_ge(sem["cRsA"], 1)
                if it == DT:
                    q.wait_ge(sem["cRsB"], 1)
                if it >= 4:
                    q.wait_ge(sem["sA"], M[("Cb", it - 4)])
                    q.wait_ge(sem[f"dCo{p}"], M[("Co", it - 4)])
                inc(f"dCl{p}", q.dma_start(
                    out=c_rs[p], in_=rsout[j2].ap()[d * HD:(d + 1) * HD, :]))
                inc(f"dCl{p}", q.dma_start(
                    out=c_x[p],
                    in_=xT[d * HD:(d + 1) * HD, j2 * BLK:(j2 + 1) * BLK]))
                M[("Cl", it)] = C[f"dCl{p}"]
            block.sync(_)

            def _(v, d=d, it=it, p=p):
                v.wait_ge(sem[f"dCl{p}"], M[("Cl", it)])
                M[("Ca", it)] = inc("sV", v.tensor_add(c_o[p], c_rs[p], c_x[p]))
            block.vector(_)

            def _(sc, d=d, j2=j2, it=it, p=p):
                sc.wait_ge(sem["sV"], M[("Ca", it)])
                M[("Cb", it)] = inc("sA", sc.activation(
                    c_o[p], c_o[p], AF.Identity,
                    bias=bo_sb[:, d:d + 1], scale=1.0))
                sc.wait_ge(sem["sA"], M[("Cb", it)])
                ins = sc.dma_start(
                    out=outT[d * HD:(d + 1) * HD, j2 * BLK:(j2 + 1) * BLK],
                    in_=c_o[p])
                M[("Co", it)] = inc(f"dCo{p}", ins)
            block.scalar(_)
            it += 1

    if debug:
        def _(q):
            for nm, (dst, src) in dbg.items():
                inc("dDbg", q.dma_start(out=dst, in_=src.ap()))
        block.sync(_)

    def _(q):
        for k in SEM_NAMES:
            if (k.startswith("d") or k.startswith("c")) and C[k]:
                q.wait_ge(sem[k], C[k])
    block.sync(_)

    block.__exit__(None, None, None)
    return nc


def prep_inputs(L0_attn_h, L0_attn_Wq, L0_attn_bq, L0_attn_Wk, L0_attn_bk,
                L0_attn_Wv, L0_attn_bv, L0_attn_Wo, L0_attn_bo,
                L0_ffn_gate_w, L0_ffn_gate_proj, L0_ffn_up_proj, L0_ffn_down_proj):
    x = np.ascontiguousarray(np.asarray(L0_attn_h, dtype=np.float32).reshape(NTOK, D).T)
    wqT = np.ascontiguousarray(np.asarray(L0_attn_Wq).T)
    wkT = np.ascontiguousarray(np.asarray(L0_attn_Wk).T)
    wvT = np.ascontiguousarray(np.asarray(L0_attn_Wv).T)
    woT = np.ascontiguousarray(np.asarray(L0_attn_Wo).T)
    gwT = np.ascontiguousarray(np.asarray(L0_ffn_gate_w).T)
    bq = np.asarray(L0_attn_bq); bk = np.asarray(L0_attn_bk)
    bv = np.asarray(L0_attn_bv); bo = np.asarray(L0_attn_bo)
    gp = np.asarray(L0_ffn_gate_proj); up = np.asarray(L0_ffn_up_proj)
    dp = np.asarray(L0_ffn_down_proj)

    mk = np.zeros((4, 128, BLK), dtype=BFNP)
    for m in range(4):
        i = np.arange(128)[:, None]
        jj = np.arange(BLK)[None, :]
        mk[m] = (128 * m + i <= jj).astype(BFNP)
    idf = np.eye(128, dtype=np.float32)

    in_maps = []
    for c in range(NCORE):
        hsl = slice(2 * c * HD, (2 * c + 2) * HD)
        sel = np.zeros((8, 128), np.float32)
        sel[c, :] = 1.0
        in_maps.append({
            "xT": np.ascontiguousarray(x[:, c * TPC:(c + 1) * TPC]),
            "wqT": np.ascontiguousarray(wqT[:, hsl]),
            "wkT": np.ascontiguousarray(wkT[:, hsl]),
            "wvT": np.ascontiguousarray(wvT[:, hsl]),
            "woT": np.ascontiguousarray(woT[hsl, :]),
            "bqh": np.ascontiguousarray(bq[hsl].reshape(NHC, HD).T),
            "bkh": np.ascontiguousarray(bk[hsl].reshape(NHC, HD).T),
            "bvr": np.ascontiguousarray(bv[hsl].reshape(1, NHC * HD)),
            "boh": np.ascontiguousarray(bo.reshape(DT, 128).T),
            "gwT": gwT,
            "gTd": np.ascontiguousarray(gp[c].T),
            "uTd": np.ascontiguousarray(up[c].T),
            "dTd": np.ascontiguousarray(dp[c].T),
            "selb": sel,
            "masks": mk,
            "idf32": idf,
        })
    return in_maps


_CACHE = {}


def kernel(**inputs):
    if "nc" not in _CACHE:
        _CACHE["nc"] = build_program(debug=False)
    nc = _CACHE["nc"]
    in_maps = prep_inputs(**inputs)
    res = run_bass_kernel_spmd(nc, in_maps, core_ids=list(range(NCORE)))
    shards = [r["outT"] for r in res.results]
    outT = np.concatenate(shards, axis=1)
    return np.ascontiguousarray(outT.T).reshape(2, S, D).astype(np.float32)
